# version 03
# baseline (speedup 1.0000x reference)
import sys
sys.path.insert(0, "/opt/trn_rl_repo")
import numpy as np
import ml_dtypes
from contextlib import ExitStack

import concourse.bass as bass
import concourse.mybir as mybir
from concourse.bass_utils import run_bass_kernel_spmd

F32 = mybir.dt.float32
BF16 = mybir.dt.bfloat16
AF = mybir.ActivationFunctionType
ALU = mybir.AluOpType
BF = ml_dtypes.bfloat16

B, S, D = 2, 8192, 2048
TOK = 2048
NKC = 16
SCALE = 1.0 / float(np.sqrt(128.0))
NEGC = -8.0
E1 = float(np.exp(1.0))
GK = 0.7978845608028654
GA = 0.044715


def _rope_cs(positions):
    inv = 1.0 / 10000.0 ** (np.arange(0, 128, 2, dtype=np.float64) / 128.0)
    ang = np.outer(inv, np.asarray(positions, np.float64))
    cos = np.concatenate([np.cos(ang), np.cos(ang)], 0).astype(np.float32)
    sin = np.concatenate([-np.sin(ang), np.sin(ang)], 0).astype(np.float32)
    return np.ascontiguousarray(cos), np.ascontiguousarray(sin)


def _rope_cs_swapped(positions):
    inv = 1.0 / 10000.0 ** (np.arange(0, 128, 2, dtype=np.float64) / 128.0)
    ang = np.outer(inv, np.asarray(positions, np.float64))
    cos = np.concatenate([np.cos(ang), np.cos(ang)], 0).astype(np.float32)
    sin = np.concatenate([np.sin(ang), -np.sin(ang)], 0).astype(np.float32)
    return np.ascontiguousarray(cos), np.ascontiguousarray(sin)


def _col_table(vec, nchunk):
    return np.ascontiguousarray(
        np.asarray(vec, np.float32).reshape(nchunk, 128).T)


def _pack_lhst(wT):
    Kin, M = wT.shape
    return np.ascontiguousarray(
        np.asarray(wT, np.float32).reshape(Kin // 128, 128, M)
        .transpose(1, 0, 2).reshape(128, -1))


_MASK_E = np.where(np.triu(np.ones((128, 128), bool)), np.float32(E1),
                   np.float32(1.0))
_MASK01 = np.triu(np.ones((128, 128), np.float32))
_EYE = np.eye(128, dtype=np.float32)


class Sem:
    __slots__ = ("h", "n")
    def __init__(self, h):
        self.h = h
        self.n = 0
    def inc(self, ins, k=1):
        ins.then_inc(self.h, k)
        self.n += k
        return ins


class Prog:
    def __init__(self):
        self.nc = bass.Bass("TRN2", target_bir_lowering=False)
        self.ctx = ExitStack()
        self._names = {}
        self._sems = {}

    def _nm(self, base):
        i = self._names.get(base, 0)
        self._names[base] = i + 1
        return base if i == 0 else f"{base}_{i}"

    def sb(self, base, shape, dt, stack=None):
        return (stack or self.ctx).enter_context(
            self.nc.sbuf_tensor(self._nm(base), shape, dt))

    def psum(self, base, shape, dt=F32, stack=None):
        return (stack or self.ctx).enter_context(
            self.nc.psum_tensor(self._nm(base), shape, dt))

    def sem(self, base, stack=None):
        if base not in self._sems:
            self._sems[base] = Sem(self.ctx.enter_context(
                self.nc.semaphore(base)))
        return self._sems[base]

    def din(self, name, shape, dt):
        return self.nc.dram_tensor(name, shape, dt, kind="ExternalInput").ap()

    def dout(self, name, shape, dt):
        return self.nc.dram_tensor(name, shape, dt, kind="ExternalOutput").ap()

    def dtmp(self, name, shape, dt):
        return self.nc.dram_tensor(name, shape, dt, kind="Internal").ap()

    def block(self):
        return self.nc.Block()


def emit_consts(p, loads, memsets=()):
    ld = p.sem("c_ld")
    with p.block() as blk:
        @blk.sync
        def _(sy):
            for dst, src, cast in loads:
                if not cast:
                    ld.inc(sy.dma_start(dst, src), 16)
        @blk.gpsimd
        def _(g):
            for dst, src, cast in loads:
                if cast:
                    ld.inc(g.dma_start(dst, src), 16)
            for ap, val in memsets:
                g.memset(ap, val)
        @blk.sync
        def _(sy):
            sy.wait_ge(ld.h, ld.n)
        @blk.gpsimd
        def _(g):
            g.wait_ge(ld.h, ld.n)


def emit_ln(p, blk, *, s_tiles, sq_buf, rows, ones_f32, ones_row, eps_tile,
            ps_stat, ps_bc0, ps_bc1, base_wait, writer,
            gamma_col=None, beta_col=None, ntok=512):
    nch = len(s_tiles)
    sq = p.sem("ln_sq")
    st = p.sem("ln_st")
    row = p.sem("ln_row")
    bc = p.sem("ln_bc")
    st_base = st.n

    @blk.scalar
    def _(s):
        s.wait_ge(base_wait[0].h, base_wait[1])
        for m in range(nch):
            if m >= 2:
                s.wait_ge(st.h, st_base + m - 1)
            sq.inc(s.square(sq_buf[m % 2][:, 0:ntok], s_tiles[m]), 1)

    @blk.tensor
    def _(t):
        sq_base = sq.n - nch
        t.wait_ge(base_wait[0].h, base_wait[1])
        for m in range(nch):
            t.matmul(ps_stat[0:1, 0:ntok], ones_f32[:, 0:1], s_tiles[m],
                     start=(m == 0), stop=(m == nch - 1))
        for m in range(nch):
            t.wait_ge(sq.h, sq_base + m + 1)
            ins = t.matmul(ps_stat[32:33, 0:ntok], ones_f32[:, 0:1],
                           sq_buf[m % 2][:, 0:ntok],
                           start=(m == 0), stop=(m == nch - 1))
            st.inc(ins, 1)

    @blk.vector
    def _(v):
        v.wait_ge(st.h, st.n)
        v.tensor_scalar(rows["mrow"][0:1, 0:ntok], ps_stat[0:1, 0:ntok],
                        1.0 / D, None, ALU.mult)
        ins = v.tensor_scalar(rows["vrow"][0:1, 0:ntok],
                              ps_stat[32:33, 0:ntok], 1.0 / D, None, ALU.mult)
        row.inc(ins, 1)

    @blk.scalar
    def _(s):
        s.wait_ge(row.h, row.n)
        row.inc(s.square(rows["m2"][0:1, 0:ntok], rows["mrow"][0:1, 0:ntok]), 1)

    @blk.vector
    def _(v):
        v.wait_ge(row.h, row.n)
        ins = v.tensor_tensor(rows["vrow"][0:1, 0:ntok],
                              rows["vrow"][0:1, 0:ntok],
                              rows["m2"][0:1, 0:ntok], ALU.subtract)
        row.inc(ins, 1)

    @blk.scalar
    def _(s):
        s.wait_ge(row.h, row.n)
        s.activation(rows["rr"][0:1, 0:ntok], rows["vrow"][0:1, 0:ntok],
                     AF.Ln, bias=eps_tile[0:1, 0:1], scale=1.0)
        ins = s.activation(rows["rr"][0:1, 0:ntok], rows["rr"][0:1, 0:ntok],
                           AF.Exp, scale=-0.5)
        row.inc(ins, 1)

    @blk.tensor
    def _(t):
        t.wait_ge(row.h, row.n)
        t.matmul(ps_bc0[:, 0:ntok], ones_row[0:1, :],
                 rows["mrow"][0:1, 0:ntok], start=True, stop=True)
        ins = t.matmul(ps_bc1[:, 0:ntok], ones_row[0:1, :],
                       rows["rr"][0:1, 0:ntok], start=True, stop=True)
        bc.inc(ins, 1)

    @blk.vector
    def _(v):
        v.wait_ge(bc.h, bc.n)
        for m in range(nch):
            tmp = sq_buf[m % 2][:, 0:ntok]
            v.tensor_tensor(tmp, s_tiles[m], ps_bc0[:, 0:ntok], ALU.subtract)
            v.tensor_tensor(tmp, tmp, ps_bc1[:, 0:ntok], ALU.mult)
            if gamma_col is not None:
                v.tensor_scalar(tmp, tmp, gamma_col(m), beta_col(m),
                                ALU.mult, ALU.add)
            writer(v, m, tmp)



def alloc_attn_scratch(p, ps_sc):
    at = {"ps_sc": ps_sc}
    at["qbuf"] = p.sb("at_qbuf", [128, 512], BF16)
    at["expS"] = [p.sb("at_expS", [128, 1024], BF16) for _ in range(2)]
    at["fina"] = p.sb("at_fina", [128, 129], F32)
    at["finb"] = p.sb("at_finb", [128, 129], F32)
    at["dvec"] = p.sb("at_dvec", [128, 1], F32)
    at["cbf"] = [p.sb("at_cbf", [128, 128], BF16) for _ in range(2)]
    at["ctxq"] = p.sb("at_ctxq", [128, 512], BF16)
    at["negc"] = p.sb("at_negc", [128, 1], F32)
    at["e1"] = p.sb("at_e1", [128, 1], F32)
    at["eye"] = p.sb("at_eye", [128, 128], BF16)
    at["mask_e"] = p.sb("at_maske", [128, 128], BF16)
    at["mask_01"] = p.sb("at_mask01", [128, 128], BF16)
    at["ps_ctx"] = [p.psum("ps_ctx", [128, 512]) for _ in range(3)]
    return at


def emit_attention(p, *, kind, qT_dram, k_res, v_res, ctx_out, at):
    shared_kv = (kind == "l0a")

    def ctx_tile(i):
        return at["ps_ctx"][i // 3][:, (i % 3) * 129:(i % 3) * 129 + 129]

    for pr in range(4):
        kvb = 0 if shared_kv else pr * S
        vvb = 0 if shared_kv else pr * 64 * 130
        for qc in range(16):
            kbs = (list(range(qc * 4 + 4)) if kind == "causal"
                   else list(range(64)))
            n = len(kbs)
            j0_of = {kb: (max(0, kb - qc * 4) if kind == "causal" else 0)
                     for kb in kbs}
            diag_of = {kb: (kb - qc * 4
                            if (kind in ("l0a", "causal")
                                and 0 <= kb - qc * 4 < 4) else None)
                       for kb in kbs}
            diag_kbs = [kb for kb in kbs if diag_of[kb] is not None]
            pv_sched = {j: [] for j in range(4)}
            for kb in kbs:
                for j in range(j0_of[kb], 4):
                    if kind == "causal" and kb > qc * 4 + j:
                        continue
                    cls = 1 if (kind == "l0a" and kb < qc * 4 + j) else 0
                    pv_sched[j].append((kb, cls))

            ld = p.sem("a_ld")
            scm = p.sem("a_sc")
            expd = p.sem("a_exp")
            vmm = p.sem("a_vmm")
            pvd = p.sem("a_pv")
            fin = p.sem("a_fin")
            trs = p.sem("a_tr")
            cpd = p.sem("a_cp")
            sto = p.sem("a_st")
            ctxz = p.sem("a_cz")
            qbuf, expS = at["qbuf"], at["expS"]

            with p.block() as blk:
                @blk.sync
                def _(sy):
                    ld.inc(sy.dma_start(
                        qbuf[:, :], qT_dram[pr * 128:pr * 128 + 128,
                                            qc * 512:qc * 512 + 512]), 16)

                @blk.tensor
                def _(t):
                    t.wait_ge(ld.h, ld.n)
                    expd_base = expd.n
                    vmm_base = vmm.n
                    ctxz_base = ctxz.n

                    def score(i):
                        kb = kbs[i]
                        j0 = j0_of[kb]
                        base = (i % 2) * 512
                        ins = t.matmul(
                            at["ps_sc"][(i // 2) % 2][:, base + j0 * 128:
                                                      base + 512],
                            k_res[:, kvb + kb * 128:kvb + kb * 128 + 128],
                            qbuf[:, j0 * 128:512], start=True, stop=True)
                        scm.inc(ins, 1)

                    def pv(i):
                        kb = kbs[i]
                        base = (i % 2) * 512
                        ins = None
                        for j in range(j0_of[kb], 4):
                            if kind == "causal" and kb > qc * 4 + j:
                                continue
                            cls = 1 if (kind == "l0a" and kb < qc * 4 + j) else 0
                            ins = t.matmul(
                                ctx_tile(j * 2 + cls),
                                expS[(i // 2) % 2][:, base + j * 128:
                                                   base + j * 128 + 128],
                                v_res[:, vvb + kb * 130:vvb + kb * 130 + 129],
                                start=False, stop=False,
                                skip_group_check=True)
                        pvd.inc(ins, 1)

                    for i in range(min(4, n)):
                        score(i)
                    ndiag_seen = 0
                    for g in range(n // 2):
                        i0, i1 = 2 * g, 2 * g + 1
                        t.wait_ge(expd.h, expd_base + g + 1)
                        if g == 0:
                            t.wait_ge(ctxz.h, ctxz_base + 1)
                        nd = sum(1 for i in (i0, i1)
                                 if diag_of[kbs[i]] is not None)
                        if nd:
                            ndiag_seen += nd
                            t.wait_ge(vmm.h, vmm_base + ndiag_seen)
                        pv(i0)
                        pv(i1)
                        if i1 + 4 < n:
                            score(i1 + 3)
                            score(i1 + 4)

                @blk.scalar
                def _(s):
                    scm_base = scm.n - n
                    pvd_base = pvd.n - n
                    for g in range(n // 2):
                        i0, i1 = 2 * g, 2 * g + 1
                        s.wait_ge(scm.h, scm_base + i1 + 1)
                        if g >= 2:
                            s.wait_ge(pvd.h, pvd_base + i0 - 2)
                        j0a = j0_of[kbs[i0]]
                        ins = s.activation(
                            expS[g % 2][:, j0a * 128:1024],
                            at["ps_sc"][g % 2][:, j0a * 128:1024],
                            AF.Exp, bias=at["negc"][:, 0:1], scale=SCALE)
                        expd.inc(ins, 1)

                @blk.vector
                def _(v):
                    expd_base = expd.n - n // 2
                    v.memset(at["ps_ctx"][0][:, :], 0.0)
                    v.memset(at["ps_ctx"][1][:, :], 0.0)
                    ctxz.inc(v.memset(at["ps_ctx"][2][:, :], 0.0), 1)
                    mask = at["mask_e"] if kind == "l0a" else at["mask_01"]
                    for di, kb in enumerate(diag_kbs):
                        i = kbs.index(kb)
                        j = diag_of[kb]
                        v.wait_ge(expd.h, expd_base + i // 2 + 1)
                        sl = expS[(i // 2) % 2][:, (i % 2) * 512 + j * 128:
                                                (i % 2) * 512 + j * 128 + 128]
                        ins = v.tensor_tensor(sl, sl, mask[:, :], ALU.mult)
                        vmm.inc(ins, 1)

            rcp = p.sem("a_rc")
            with p.block() as blk2:
                @blk2.vector
                def _(v):
                    trs_base = trs.n

                    def fin_j(j):
                        has_b = any(c == 1 for _, c in pv_sched[j])
                        fb = at["finb"]
                        if j >= 2:
                            v.wait_ge(trs.h, trs_base + j - 1)
                        if has_b:
                            v.tensor_copy(at["fina"][:, :], ctx_tile(j * 2 + 1))
                            ins = v.scalar_tensor_tensor(
                                fb[:, :], at["fina"][:, :], at["e1"][:, 0:1],
                                ctx_tile(j * 2), ALU.mult, ALU.add)
                        else:
                            ins = v.tensor_copy(fb[:, :], ctx_tile(j * 2))
                        rcp.inc(ins, 1)
                        v.wait_ge(rcp.h, rcp.n)
                        rcp.inc(v.reciprocal(at["dvec"][:, :],
                                             fb[:, 128:129]), 1)
                        v.wait_ge(rcp.h, rcp.n)
                        ins = v.tensor_scalar(at["cbf"][j % 2][:, :],
                                              fb[:, 0:128], at["dvec"][:, 0:1],
                                              None, ALU.mult)
                        fin.inc(ins, 1)

                    def copy_j(j):
                        v.wait_ge(trs.h, trs_base + j + 1)
                        ins = v.tensor_copy(
                            at["ctxq"][:, j * 128:j * 128 + 128],
                            at["ps_tr"][:, 0:128])
                        cpd.inc(ins, 1)

                    fin_j(0)
                    fin_j(1)
                    copy_j(0)
                    fin_j(2)
                    copy_j(1)
                    fin_j(3)
                    copy_j(2)
                    copy_j(3)

                @blk2.tensor
                def _(t):
                    fin_base = fin.n - 4
                    cpd_base = cpd.n - 4
                    for j in range(4):
                        t.wait_ge(fin.h, fin_base + j + 1)
                        if j >= 1:
                            t.wait_ge(cpd.h, cpd_base + j)
                        ins = t.transpose(at["ps_tr"][:, 0:128],
                                          at["cbf"][j % 2][:, :],
                                          at["eye"][:, :])
                        trs.inc(ins, 1)

                @blk2.sync
                def _(sy):
                    sy.wait_ge(cpd.h, cpd.n)
                    sto.inc(sy.dma_start(
                        ctx_out[pr * 128:pr * 128 + 128,
                                qc * 512:qc * 512 + 512], at["ctxq"][:, :]), 16)
                    sy.wait_ge(sto.h, sto.n)



def emit_proj(p, *, tgs, n_k, n_m, w_tile, rhs_tile, load_rhs, evac, ps_pool,
              ntok=512):
    for tg in tgs:
        sems = {"ld": p.sem("pr_ld"), "mm": p.sem("pr_mm"),
                "ev": p.sem("pr_ev")}
        with p.block() as blk:
            @blk.sync
            def _(sy, tg=tg, sems=sems):
                load_rhs(sy, None, tg, sems)
            @blk.gpsimd
            def _(g, tg=tg, sems=sems):
                load_rhs(None, g, tg, sems)
            @blk.tensor
            def _(t, tg=tg, sems=sems):
                t.wait_ge(sems["ld"].h, sems["ld"].n)
                ev_base = sems["ev"].n
                for mi in range(n_m):
                    ps = ps_pool[mi % 2]
                    if mi >= 2:
                        t.wait_ge(sems["ev"].h, ev_base + mi - 1)
                    ins = None
                    for k in range(n_k):
                        ins = t.matmul(ps[:, 0:ntok], w_tile(k, mi),
                                       rhs_tile(k, tg),
                                       start=(k == 0), stop=(k == n_k - 1))
                    sems["mm"].inc(ins, 1)
            evac(blk, tg, sems, ps_pool)



def build_attn_launch(kind, debug=False, upto="attn"):
    _phases = ["ln1", "q", "k", "v", "attn"]
    _lim = _phases.index(upto)
    p = Prog()
    kw = 128 if kind == "l0a" else 512
    nm_kv = kw // 128
    xt = p.din("xt", [D, S], F32)
    wq_p = p.din("wq_p", [128, NKC * 512], F32)
    wk_p = p.din("wk_p", [128, NKC * kw], F32)
    wv_p = p.din("wv_p", [128, NKC * kw], F32)
    eye_in = p.din("eye", [128, 128], F32)
    maske_in = p.din("maske", [128, 128], F32)
    mask01_in = p.din("mask01", [128, 128], F32)
    if kind in ("l0a", "l1"):
        bq_in = p.din("bq_t", [128, 4], F32)
        bk_in = p.din("bk_t", [128, nm_kv], F32)
        bvb_in = p.din("bvb", [128, kw], F32)
    if kind == "l0a":
        rqc_in = p.din("rq_cos", [128, 4], F32)
        rqs_in = p.din("rq_sin", [128, 4], F32)
        rkc_in = p.din("rk_cos", [128, 1], F32)
        rks_in = p.din("rk_sin", [128, 1], F32)
    if kind == "l1":
        rpc_in = p.din("rp_cos", [128, S], F32)
        rps_in = p.din("rp_sin", [128, S], F32)
    if kind == "inner":
        g1_in = p.din("g1_t", [128, 16], F32)
        b1_in = p.din("b1_t", [128, 16], F32)
    ctx_out = p.dout("ctxT", [512, S], BF16)
    kw_ = 128 if kind == "l0a" else 512
    if debug:
        qT = p.dout("qT_s", [512, S], BF16)
        kdbg = p.dout("kdbg", [128, (kw_ // 128) * S], BF16)
        vdbg = p.dout("vdbg", [128, (kw_ // 128) * 64 * 130], BF16)
    else:
        qT = p.dtmp("qT_s", [512, S], BF16)
    if kind == "inner":
        xn1 = p.dtmp("xn1_s", [D, S], BF16)

    k_res = p.sb("k_res", [128, nm_kv * S], BF16)
    v_res = p.sb("v_res", [128, nm_kv * 64 * 130], BF16)
    ones_f32 = p.sb("ones_f32", [128, 1], F32)
    ones_row = p.sb("ones_row", [1, 128], F32)
    eps_t = p.sb("eps_t", [128, 1], F32)
    ps_w = [p.psum("ps_w", [128, 1024]) for _ in range(2)]
    ps_mm = [ps_w[0][:, 0:512], ps_w[1][:, 0:512]]
    at = alloc_attn_scratch(p, ps_mm)
    more = {}
    consts = [(at["eye"][:, :], eye_in, True),
              (at["mask_e"][:, :], maske_in, True),
              (at["mask_01"][:, :], mask01_in, True)]
    msets = [(at["negc"][:, 0:1], NEGC), (at["e1"][:, 0:1], E1),
             (ones_f32[:, 0:1], 1.0), (ones_row[0:1, :], 1.0),
             (eps_t[:, 0:1], 1e-5), (v_res[:, :], 1.0)]
    if kind in ("l0a", "l1"):
        more["bq"] = p.sb("c_bq", [128, 4], F32)
        more["bk"] = p.sb("c_bk", [128, nm_kv], F32)
        more["bvb"] = p.sb("c_bvb", [128, kw], F32)
        consts += [(more["bq"][:, :], bq_in, False),
                   (more["bk"][:, :], bk_in, False),
                   (more["bvb"][:, :], bvb_in, False)]
    if kind == "l0a":
        for nmv, src in (("rq_cos", rqc_in), ("rq_sin", rqs_in),
                         ("rk_cos", rkc_in), ("rk_sin", rks_in)):
            more[nmv] = p.sb("c_" + nmv, list(src.shape), F32)
            consts.append((more[nmv][:, :], src, False))
    if kind == "l1":
        more["rc"] = [p.sb("c_rc", [128, 512], F32) for _ in range(2)]
        more["rs"] = [p.sb("c_rs", [128, 512], F32) for _ in range(2)]
    if kind == "inner":
        more["g1"] = p.sb("c_g1", [128, 16], F32)
        more["b1"] = p.sb("c_b1", [128, 16], F32)
        consts += [(more["g1"][:, :], g1_in, False),
                   (more["b1"][:, :], b1_in, False)]
    emit_consts(p, consts, msets)

    with ExitStack() as phase_stack:
        xb = [p.sb("xb", [128, 512], BF16, phase_stack) for _ in range(NKC)]
        q0 = p.sb("q0", [128, 512], F32, phase_stack)
        rt = p.sb("rt", [128, 512], F32, phase_stack)
        qex = [p.sb("qex", [128, 512], BF16, phase_stack) for _ in range(2)]

        def load_x(sy, g, tg, sems):
            if kind != "inner" and g is not None:
                for k in range(NKC):
                    sems["ld"].inc(g.dma_start(
                        xb[k][:, :],
                        xt[k * 128:k * 128 + 128, tg * 512:tg * 512 + 512]), 16)
            if kind == "inner" and sy is not None:
                for k in range(NKC):
                    sems["ld"].inc(sy.dma_start(
                        xb[k][:, :],
                        xn1[k * 128:k * 128 + 128, tg * 512:tg * 512 + 512]), 16)

        if kind == "inner":
            with ExitStack() as st1:
                xf = [p.sb("xf", [128, 512], F32, st1) for _ in range(NKC)]
                sqb = [p.sb("ln_sq", [128, 512], F32, st1) for _ in range(2)]
                rows = {k: p.sb("ln_rows", [33, 512], F32, st1)
                        for k in ("mrow", "vrow", "m2", "rr")}
                ps_stat = p.psum("ps_stat", [33, 512], stack=st1)
                for tg in range(16):
                    ldf = p.sem("lnf_ld")
                    stf = p.sem("lnf_st")
                    with p.block() as blk:
                        @blk.sync
                        def _(sy, tg=tg, ldf=ldf):
                            for k in range(NKC):
                                ldf.inc(sy.dma_start(
                                    xf[k][:, :],
                                    xt[k * 128:k * 128 + 128,
                                       tg * 512:tg * 512 + 512]), 16)

                        def writer(v, m, tmp, stf=stf):
                            stf.inc(v.tensor_copy(xb[m][:, :], tmp), 1)

                        emit_ln(p, blk,
                                s_tiles=[xf[k][:, :] for k in range(NKC)],
                                sq_buf=sqb, rows=rows, ones_f32=ones_f32,
                                ones_row=ones_row, eps_tile=eps_t,
                                ps_stat=ps_stat, ps_bc0=at["ps_ctx"][0],
                                ps_bc1=at["ps_ctx"][1], base_wait=(ldf, ldf.n),
                                writer=writer,
                                gamma_col=lambda m: more["g1"][:, m:m + 1],
                                beta_col=lambda m: more["b1"][:, m:m + 1])

                        @blk.sync
                        def _(sy, tg=tg, stf=stf):
                            sy.wait_ge(stf.h, stf.n)
                            for k in range(NKC):
                                stf.inc(sy.dma_start(
                                    xn1[k * 128:k * 128 + 128,
                                        tg * 512:tg * 512 + 512],
                                    xb[k][:, :]), 16)
                            sy.wait_ge(stf.h, stf.n)

        if _lim < 1:
            return p
        with ExitStack() as stq_s:
            wq_sb = p.sb("wq_sb", [128, NKC * 512], BF16, stq_s)
            emit_consts(p, [(wq_sb[:, :], wq_p, True)])

            def evac_q(blk, tg, sems, pool):
                stq = p.sem("q_st")

                @blk.sync
                def _(sy, tg=tg, sems=sems):
                    if kind == "l1":
                        b = tg % 2
                        sems["ld"].inc(sy.dma_start(
                            more["rc"][b][:, :],
                            rpc_in[:, tg * 512:tg * 512 + 512]), 16)
                        sems["ld"].inc(sy.dma_start(
                            more["rs"][b][:, :],
                            rps_in[:, tg * 512:tg * 512 + 512]), 16)

                @blk.vector
                def _(v, tg=tg, sems=sems, stq=stq):
                    if kind == "l1":
                        v.wait_ge(sems["ld"].h, sems["ld"].n)
                    mm_base = sems["mm"].n - 4
                    stq_base = stq.n
                    for mi in range(4):
                        v.wait_ge(sems["mm"].h, mm_base + mi + 1)
                        if mi >= 2:
                            v.wait_ge(stq.h, stq_base + 16 * (mi - 1))
                        ps = pool[mi % 2]
                        qx = qex[mi % 2]
                        if kind == "l0a":
                            v.tensor_scalar(q0[:, :], ps[:, 0:512],
                                            more["bq"][:, mi:mi + 1], None,
                                            ALU.add)
                            v.tensor_scalar(rt[0:64, :], q0[64:128, :],
                                            more["rq_sin"][0:64, mi:mi + 1],
                                            None, ALU.mult)
                            v.tensor_scalar(rt[64:128, :], q0[0:64, :],
                                            more["rq_sin"][64:128, mi:mi + 1],
                                            None, ALU.mult)
                            ins = v.scalar_tensor_tensor(
                                qx[:, :], q0[:, :],
                                more["rq_cos"][:, mi:mi + 1], rt[:, :],
                                ALU.mult, ALU.add)
                        elif kind == "l1":
                            b = tg % 2
                            v.tensor_scalar(q0[:, :], ps[:, 0:512],
                                            more["bq"][:, mi:mi + 1], None,
                                            ALU.add)
                            v.tensor_tensor(rt[0:64, :], q0[64:128, :],
                                            more["rs"][b][64:128, :], ALU.mult)
                            v.tensor_tensor(rt[64:128, :], q0[0:64, :],
                                            more["rs"][b][0:64, :], ALU.mult)
                            v.tensor_tensor(q0[:, :], q0[:, :],
                                            more["rc"][b][:, :], ALU.mult)
                            ins = v.tensor_tensor(qx[:, :], q0[:, :],
                                                  rt[:, :], ALU.add)
                        else:
                            ins = v.tensor_copy(qx[:, :], ps[:, 0:512])
                        sems["ev"].inc(ins, 1)

                @blk.sync
                def _(sy, tg=tg, sems=sems, stq=stq):
                    ev_base = sems["ev"].n - 4
                    for mi in range(4):
                        sy.wait_ge(sems["ev"].h, ev_base + mi + 1)
                        stq.inc(sy.dma_start(
                            qT[mi * 128:mi * 128 + 128,
                               tg * 512:tg * 512 + 512], qex[mi % 2][:, :]), 16)
                    sy.wait_ge(stq.h, stq.n)

            emit_proj(p, tgs=list(range(16)), n_k=NKC, n_m=4,
                      w_tile=lambda k, mi: wq_sb[:, k * 512 + mi * 128:
                                                 k * 512 + mi * 128 + 128],
                      rhs_tile=lambda k, tg: xb[k][:, :], load_rhs=load_x,
                      evac=evac_q, ps_pool=ps_mm)

        if _lim < 2:
            return p
        with ExitStack() as stk_s:
            wk_sb = p.sb("wk_sb", [128, NKC * kw], BF16, stk_s)
            emit_consts(p, [(wk_sb[:, :], wk_p, True)])

            def evac_k(blk, tg, sems, pool):
                @blk.sync
                def _(sy, tg=tg, sems=sems):
                    if kind == "l1":
                        b = tg % 2
                        sems["ld"].inc(sy.dma_start(
                            more["rc"][b][:, :],
                            rpc_in[:, tg * 512:tg * 512 + 512]), 16)
                        sems["ld"].inc(sy.dma_start(
                            more["rs"][b][:, :],
                            rps_in[:, tg * 512:tg * 512 + 512]), 16)

                @blk.vector
                def _(v, tg=tg, sems=sems):
                    if kind == "l1":
                        v.wait_ge(sems["ld"].h, sems["ld"].n)
                    mm_base = sems["mm"].n - nm_kv
                    for mi in range(nm_kv):
                        v.wait_ge(sems["mm"].h, mm_base + mi + 1)
                        ps = pool[mi % 2]
                        dst = k_res[:, mi * S + tg * 512:
                                    mi * S + tg * 512 + 512]
                        if kind == "l0a":
                            v.tensor_scalar(q0[:, :], ps[:, 0:512],
                                            more["bk"][:, mi:mi + 1], None,
                                            ALU.add)
                            v.tensor_scalar(rt[0:64, :], q0[64:128, :],
                                            more["rk_sin"][0:64, 0:1], None,
                                            ALU.mult)
                            v.tensor_scalar(rt[64:128, :], q0[0:64, :],
                                            more["rk_sin"][64:128, 0:1], None,
                                            ALU.mult)
                            ins = v.scalar_tensor_tensor(
                                dst, q0[:, :], more["rk_cos"][:, 0:1],
                                rt[:, :], ALU.mult, ALU.add)
                        elif kind == "l1":
                            b = tg % 2
                            v.tensor_scalar(q0[:, :], ps[:, 0:512],
                                            more["bk"][:, mi:mi + 1], None,
                                            ALU.add)
                            v.tensor_tensor(rt[0:64, :], q0[64:128, :],
                                            more["rs"][b][64:128, :], ALU.mult)
                            v.tensor_tensor(rt[64:128, :], q0[0:64, :],
                                            more["rs"][b][0:64, :], ALU.mult)
                            v.tensor_tensor(q0[:, :], q0[:, :],
                                            more["rc"][b][:, :], ALU.mult)
                            ins = v.tensor_tensor(dst, q0[:, :], rt[:, :],
                                                  ALU.add)
                        else:
                            ins = v.tensor_copy(dst, ps[:, 0:512])
                        sems["ev"].inc(ins, 1)

            emit_proj(p, tgs=list(range(16)), n_k=NKC, n_m=nm_kv,
                      w_tile=lambda k, mi: wk_sb[:, k * kw + mi * 128:
                                                 k * kw + mi * 128 + 128],
                      rhs_tile=lambda k, tg: xb[k][:, :], load_rhs=load_x,
                      evac=evac_k, ps_pool=ps_mm)

        if _lim < 3:
            return p
        with ExitStack() as stv_s:
            wv_sb = p.sb("wv_sb", [128, NKC * kw], BF16, stv_s)
            emit_consts(p, [(wv_sb[:, :], wv_p, True)])
            for tg in range(16):
                sems = {"ld": p.sem("v_ld"), "mm": p.sem("v_mm"),
                        "ev": p.sem("v_ev")}
                with p.block() as blk:
                    @blk.sync
                    def _(sy, tg=tg, sems=sems):
                        load_x(sy, None, tg, sems)
                    @blk.gpsimd
                    def _(g, tg=tg, sems=sems):
                        load_x(None, g, tg, sems)
                    @blk.tensor
                    def _(t, tg=tg, sems=sems):
                        t.wait_ge(sems["ld"].h, sems["ld"].n)
                        ev_base = sems["ev"].n
                        for tt in range(4):
                            ps = ps_mm[tt % 2]
                            if tt >= 2:
                                t.wait_ge(sems["ev"].h, ev_base + tt - 1)
                            ins = None
                            for k in range(NKC):
                                ins = t.matmul(
                                    ps[:, 0:kw],
                                    xb[k][:, tt * 128:tt * 128 + 128],
                                    wv_sb[:, k * kw:(k + 1) * kw],
                                    start=(k == 0), stop=(k == NKC - 1))
                            sems["mm"].inc(ins, 1)
                    @blk.vector
                    def _(v, tg=tg, sems=sems):
                        mm_base = sems["mm"].n - 4
                        for tt in range(4):
                            v.wait_ge(sems["mm"].h, mm_base + tt + 1)
                            ps = ps_mm[tt % 2]
                            kb = tg * 4 + tt
                            ins = None
                            for j4 in range(nm_kv):
                                dst = v_res[:, j4 * 64 * 130 + kb * 130:
                                            j4 * 64 * 130 + kb * 130 + 128]
                                if kind == "inner":
                                    ins = v.tensor_copy(
                                        dst, ps[:, j4 * 128:j4 * 128 + 128])
                                else:
                                    ins = v.tensor_tensor(
                                        dst, ps[:, j4 * 128:j4 * 128 + 128],
                                        more["bvb"][:, j4 * 128:j4 * 128 + 128],
                                        ALU.add)
                            sems["ev"].inc(ins, 1)

    if _lim < 4:
        return p
    if debug:
        dsem = p.sem("dbg_st")
        with p.block() as blk:
            @blk.sync
            def _(sy):
                dsem.inc(sy.dma_start(kdbg, k_res[:, :]), 16)
                dsem.inc(sy.dma_start(vdbg, v_res[:, :]), 16)
                sy.wait_ge(dsem.h, dsem.n)

    at["ps_tr"] = p.psum("ps_tr", [128, 1024], BF16)
    at["ps_sc"] = ps_w
    emit_attention(p, kind=("causal" if kind == "inner"
                            else ("l0a" if kind == "l0a" else "full")),
                   qT_dram=qT, k_res=k_res, v_res=v_res, ctx_out=ctx_out,
                   at=at)
    return p


def build_projln_launch(kind):
    p = Prog()
    xo = p.din("xo", [D, TOK], F32)
    w_p = p.din("w_p", [128, NKC * D], F32)
    b_in = p.din("b_t", [128, 16], F32)
    if kind == "proj":
        cin = p.din("cin", [D, TOK], BF16)
    else:
        s_in = p.din("s_t", [128, 16], F32)
    y = p.dout("y", [D, TOK], F32)

    w_sb = p.sb("w_sb", [128, NKC * D], BF16)
    rhs = [p.sb("rhs", [128, 512], BF16) for _ in range(NKC)]
    xf = [p.sb("xf", [128, 512], F32) for _ in range(NKC)]
    sti = [p.sb("sti", [128, 512], F32) for _ in range(NKC)]
    sqb = [p.sb("ln_sq", [128, 512], F32) for _ in range(2)]
    rows = {k: p.sb("ln_rows", [33, 512], F32)
            for k in ("mrow", "vrow", "m2", "rr")}
    outb = [p.sb("outb", [128, 512], F32) for _ in range(2)]
    ones_f32 = p.sb("ones_f32", [128, 1], F32)
    ones_row = p.sb("ones_row", [1, 128], F32)
    eps_t = p.sb("eps_t", [128, 1], F32)
    btl = p.sb("btl", [128, 16], F32)
    stl = p.sb("stl", [128, 16], F32)
    ps_mm = [p.psum("ps_mm", [128, 512]) for _ in range(2)]
    ps_stat = p.psum("ps_stat", [33, 512])
    ps_bc = [p.psum("ps_bc", [128, 512]) for _ in range(2)]

    consts = [(w_sb[:, :], w_p, True), (btl[:, :], b_in, False)]
    if kind == "ffn2":
        consts.append((stl[:, :], s_in, False))
    emit_consts(p, consts, [(ones_f32[:, 0:1], 1.0), (ones_row[0:1, :], 1.0),
                            (eps_t[:, 0:1], 1e-5)])

    for tg in range(4):
        sems = {"ld": p.sem("pj_ld"), "mm": p.sem("pj_mm"),
                "ev": p.sem("pj_ev"), "st": p.sem("pj_st")}
        with p.block() as blk:
            @blk.sync
            def _(sy, tg=tg, sems=sems):
                for k in range(NKC):
                    sems["ld"].inc(sy.dma_start(
                        xf[k][:, :],
                        xo[k * 128:k * 128 + 128, tg * 512:tg * 512 + 512]), 16)
                if kind == "proj":
                    for k in range(NKC):
                        sems["ld"].inc(sy.dma_start(
                            rhs[k][:, :],
                            cin[k * 128:k * 128 + 128,
                                tg * 512:tg * 512 + 512]), 16)

            @blk.gpsimd
            def _(g, tg=tg, sems=sems):
                if kind == "ffn2":
                    for k in range(NKC):
                        sems["ld"].inc(g.dma_start(
                            rhs[k][:, :],
                            xo[k * 128:k * 128 + 128,
                               tg * 512:tg * 512 + 512]), 16)

            @blk.tensor
            def _(t, tg=tg, sems=sems):
                t.wait_ge(sems["ld"].h, sems["ld"].n)
                ev_base = sems["ev"].n
                for mi in range(16):
                    ps = ps_mm[mi % 2]
                    if mi >= 2:
                        t.wait_ge(sems["ev"].h, ev_base + mi - 1)
                    ins = None
                    for k in range(NKC):
                        ins = t.matmul(ps[:, :],
                                       w_sb[:, k * D + mi * 128:
                                            k * D + mi * 128 + 128],
                                       rhs[k][:, :],
                                       start=(k == 0), stop=(k == NKC - 1))
                    sems["mm"].inc(ins, 1)

            @blk.vector
            def _(v, tg=tg, sems=sems):
                mm_base = sems["mm"].n - 16
                for mi in range(16):
                    v.wait_ge(sems["mm"].h, mm_base + mi + 1)
                    ps = ps_mm[mi % 2]
                    if kind == "proj":
                        ins = v.scalar_tensor_tensor(
                            sti[mi][:, :], ps[:, :], btl[:, mi:mi + 1],
                            xf[mi][:, :], ALU.add, ALU.add)
                    else:
                        v.tensor_scalar(sti[mi][:, :], ps[:, :],
                                        stl[:, mi:mi + 1], btl[:, mi:mi + 1],
                                        ALU.mult, ALU.add)
                        ins = v.tensor_tensor(sti[mi][:, :], sti[mi][:, :],
                                              xf[mi][:, :], ALU.add)
                    sems["ev"].inc(ins, 1)

            st_b = sems["st"].n

            def writer(v, m, tmp, sems=sems, st_b=st_b):
                if m >= 2:
                    v.wait_ge(sems["st"].h, st_b + 16 * (m - 1))
                sems["ev"].inc(v.tensor_copy(outb[m % 2][:, :], tmp), 1)

            emit_ln(p, blk, s_tiles=[sti[k][:, :] for k in range(NKC)],
                    sq_buf=sqb, rows=rows, ones_f32=ones_f32,
                    ones_row=ones_row, eps_tile=eps_t, ps_stat=ps_stat,
                    ps_bc0=ps_bc[0], ps_bc1=ps_bc[1],
                    base_wait=(sems["ev"], sems["ev"].n), writer=writer)

            @blk.sync
            def _(sy, tg=tg, sems=sems):
                ev_base = sems["ev"].n - 16
                for m in range(16):
                    sy.wait_ge(sems["ev"].h, ev_base + m + 1)
                    sems["st"].inc(sy.dma_start(
                        y[m * 128:m * 128 + 128, tg * 512:tg * 512 + 512],
                        outb[m % 2][:, :]), 16)
                sy.wait_ge(sems["st"].h, sems["st"].n)
    return p


def build_d1_launch():
    p = Prog()
    cin = p.din("cin", [D, TOK], BF16)
    xo = p.din("xo", [D, TOK], F32)
    wo_p = p.din("wo_p", [128, NKC * D], F32)
    g2_in = p.din("g2_t", [128, 16], F32)
    b2_in = p.din("b2_t", [128, 16], F32)
    wr_p = p.din("wr_p", [128, NKC * 3], F32)
    wd_p = p.din("wd_p", [128, NKC * 32], F32)
    x2i = p.dout("x2i", [D, TOK], F32)
    xn2 = p.dout("xn2", [D, TOK], BF16)
    lgr_o = p.dout("lgr", [3, TOK], F32)
    tT_o = p.dout("tT", [32, TOK], BF16)

    w_sb = p.sb("w_sb", [128, NKC * D], BF16)
    wr_sb = p.sb("wr_sb", [128, NKC * 3], F32)
    wd_sb = p.sb("wd_sb", [128, NKC * 32], BF16)
    rhs = [p.sb("rhs", [128, 512], BF16) for _ in range(NKC)]
    xf = [p.sb("xf", [128, 512], F32) for _ in range(NKC)]
    sti = [p.sb("sti", [128, 512], F32) for _ in range(NKC)]
    sqb = [p.sb("ln_sq", [128, 512], F32) for _ in range(2)]
    rows = {k: p.sb("ln_rows", [33, 512], F32)
            for k in ("mrow", "vrow", "m2", "rr")}
    outb = [p.sb("outb", [128, 512], BF16) for _ in range(2)]
    lgs = p.sb("lgs", [33, 512], F32)
    tts = p.sb("tts", [32, 512], BF16)
    g2l = p.sb("g2l", [128, 16], F32)
    b2l = p.sb("b2l", [128, 16], F32)
    ones_f32 = p.sb("ones_f32", [128, 1], F32)
    ones_row = p.sb("ones_row", [1, 128], F32)
    eps_t = p.sb("eps_t", [128, 1], F32)
    ps_mm = [p.psum("ps_mm", [128, 512]) for _ in range(2)]
    ps_stat = p.psum("ps_stat", [33, 512])
    ps_bc = [p.psum("ps_bc", [128, 512]) for _ in range(2)]
    ps_r = p.psum("ps_r", [33, 512])

    emit_consts(p, [(w_sb[:, :], wo_p, True), (wr_sb[:, :], wr_p, False),
                    (wd_sb[:, :], wd_p, True), (g2l[:, :], g2_in, False),
                    (b2l[:, :], b2_in, False)],
                [(ones_f32[:, 0:1], 1.0), (ones_row[0:1, :], 1.0),
                 (eps_t[:, 0:1], 1e-5)])

    for tg in range(4):
        sems = {"ld": p.sem("d1_ld"), "mm": p.sem("d1_mm"),
                "ev": p.sem("d1_ev"), "st": p.sem("d1_st"),
                "rt": p.sem("d1_rt")}
        with p.block() as blk:
            @blk.sync
            def _(sy, tg=tg, sems=sems):
                for k in range(NKC):
                    sems["ld"].inc(sy.dma_start(
                        xf[k][:, :],
                        xo[k * 128:k * 128 + 128, tg * 512:tg * 512 + 512]), 16)
                for k in range(NKC):
                    sems["ld"].inc(sy.dma_start(
                        rhs[k][:, :],
                        cin[k * 128:k * 128 + 128, tg * 512:tg * 512 + 512]),
                        16)

            @blk.tensor
            def _(t, tg=tg, sems=sems):
                t.wait_ge(sems["ld"].h, sems["ld"].n)
                ev_base = sems["ev"].n
                for mi in range(16):
                    ps = ps_mm[mi % 2]
                    if mi >= 2:
                        t.wait_ge(sems["ev"].h, ev_base + mi - 1)
                    ins = None
                    for k in range(NKC):
                        ins = t.matmul(ps[:, :],
                                       w_sb[:, k * D + mi * 128:
                                            k * D + mi * 128 + 128],
                                       rhs[k][:, :],
                                       start=(k == 0), stop=(k == NKC - 1))
                    sems["mm"].inc(ins, 1)

            @blk.vector
            def _(v, tg=tg, sems=sems):
                mm_base = sems["mm"].n - 16
                for mi in range(16):
                    v.wait_ge(sems["mm"].h, mm_base + mi + 1)
                    ins = v.tensor_tensor(sti[mi][:, :], ps_mm[mi % 2][:, :],
                                          xf[mi][:, :], ALU.add)
                    sems["ev"].inc(ins, 1)

            @blk.sync
            def _(sy, tg=tg, sems=sems):
                ev_base = sems["ev"].n - 16
                for mi in range(16):
                    sy.wait_ge(sems["ev"].h, ev_base + mi + 1)
                    sems["st"].inc(sy.dma_start(
                        x2i[mi * 128:mi * 128 + 128, tg * 512:tg * 512 + 512],
                        sti[mi][:, :]), 16)

            st_b = sems["st"].n

            def writer(v, m, tmp, sems=sems, st_b=st_b):
                if m >= 2:
                    v.wait_ge(sems["st"].h, st_b + 16 * (m - 1))
                sems["ev"].inc(v.tensor_copy(outb[m % 2][:, :], tmp), 1)

            emit_ln(p, blk, s_tiles=[sti[k][:, :] for k in range(NKC)],
                    sq_buf=sqb, rows=rows, ones_f32=ones_f32,
                    ones_row=ones_row, eps_tile=eps_t, ps_stat=ps_stat,
                    ps_bc0=ps_bc[0], ps_bc1=ps_bc[1],
                    base_wait=(sems["ev"], sems["ev"].n), writer=writer,
                    gamma_col=lambda m: g2l[:, m:m + 1],
                    beta_col=lambda m: b2l[:, m:m + 1])

            @blk.sync
            def _(sy, tg=tg, sems=sems):
                ev_base = sems["ev"].n - 16
                for m in range(16):
                    sy.wait_ge(sems["ev"].h, ev_base + m + 1)
                    sems["st"].inc(sy.dma_start(
                        xn2[m * 128:m * 128 + 128, tg * 512:tg * 512 + 512],
                        outb[m % 2][:, :]), 16)
                sy.wait_ge(sems["st"].h, sems["st"].n)

        with p.block() as blk:
            semr = {"ld": p.sem("dr_ld"), "mm": p.sem("dr_mm"),
                    "ev": p.sem("dr_ev")}
            @blk.gpsimd
            def _(g, tg=tg, semr=semr):
                for k in range(NKC):
                    semr["ld"].inc(g.dma_start(
                        rhs[k][:, :],
                        xo[k * 128:k * 128 + 128, tg * 512:tg * 512 + 512]), 16)

            @blk.tensor
            def _(t, tg=tg, semr=semr):
                t.wait_ge(semr["ld"].h, semr["ld"].n)
                ins = None
                for k in range(NKC):
                    ins = t.matmul(ps_r[0:3, :], wr_sb[:, k * 3:k * 3 + 3],
                                   xf[k][:, :],
                                   start=(k == 0), stop=(k == NKC - 1))
                semr["mm"].inc(ins, 1)
                for k in range(NKC):
                    ins = t.matmul(ps_mm[0][0:32, :],
                                   wd_sb[:, k * 32:k * 32 + 32], rhs[k][:, :],
                                   start=(k == 0), stop=(k == NKC - 1))
                semr["mm"].inc(ins, 1)

            @blk.vector
            def _(v, tg=tg, semr=semr):
                v.wait_ge(semr["mm"].h, semr["mm"].n)
                v.tensor_copy(lgs[0:3, :], ps_r[0:3, :])
                ins = v.tensor_copy(tts[0:32, :], ps_mm[0][0:32, :])
                semr["ev"].inc(ins, 1)

            @blk.sync
            def _(sy, tg=tg, semr=semr):
                sy.wait_ge(semr["ev"].h, semr["ev"].n)
                semr["ld"].inc(sy.dma_start(
                    lgr_o[0:3, tg * 512:tg * 512 + 512], lgs[0:3, :]), 16)
                semr["ld"].inc(sy.dma_start(
                    tT_o[0:32, tg * 512:tg * 512 + 512], tts[0:32, :]), 16)
                sy.wait_ge(semr["ld"].h, semr["ld"].n)
    return p


def build_d2_launch():
    p = Prog()
    xn2 = p.din("xn2", [D, TOK], BF16)
    ff1T = p.din("ff1T", [D, 8192], F32)
    gt = p.dout("gt", [8192, TOK], BF16)

    xr = p.sb("xr", [128, NKC * TOK], BF16)
    w1b = [[p.sb("w1b", [128, 128], BF16) for _ in range(NKC)]
           for _ in range(2)]
    gs1 = [p.sb("gs1", [128, 512], F32) for _ in range(2)]
    gs2 = [p.sb("gs2", [128, 512], F32) for _ in range(2)]
    gs3 = [p.sb("gs3", [128, 512], F32) for _ in range(2)]
    gout = [p.sb("gout", [128, 512], BF16) for _ in range(2)]
    ps_mm = [p.psum("ps_mm", [128, 512]) for _ in range(2)]

    emit_consts(p, [(xr[:, k * TOK:(k + 1) * TOK],
                     xn2[k * 128:(k + 1) * 128, :], False)
                    for k in range(NKC)])

    wl = p.sem("f1_wl")
    mm = p.sem("f1_mm")
    g1d = p.sem("f1_g1")
    v1d = p.sem("f1_v1")
    t2d = p.sem("f1_t2")
    gv = p.sem("f1_gv")
    st = p.sem("f1_st")

    with p.block() as blk:
        @blk.gpsimd
        def _(g):
            for mf in range(64):
                if mf >= 2:
                    g.wait_ge(mm.h, 4 * (mf - 1))
                for k in range(NKC):
                    wl.inc(g.dma_start(
                        w1b[mf % 2][k][:, :],
                        ff1T[k * 128:k * 128 + 128,
                             mf * 128:mf * 128 + 128]), 16)

        @blk.tensor
        def _(t):
            for mf in range(64):
                t.wait_ge(wl.h, 16 * NKC * (mf + 1))
                for tg in range(4):
                    idx = mf * 4 + tg
                    if idx >= 2:
                        t.wait_ge(gv.h, idx - 1)
                    ins = None
                    for k in range(NKC):
                        ins = t.matmul(
                            ps_mm[idx % 2][:, :], w1b[mf % 2][k][:, :],
                            xr[:, k * TOK + tg * 512:k * TOK + tg * 512 + 512],
                            start=(k == 0), stop=(k == NKC - 1))
                    mm.inc(ins, 1)

        @blk.scalar
        def _(s):
            for i in range(257):
                if i < 256:
                    s.wait_ge(mm.h, i + 1)
                    if i >= 2:
                        s.wait_ge(v1d.h, i - 1)
                    g1d.inc(s.square(gs1[i % 2][:, :], ps_mm[i % 2][:, :]), 1)
                if i >= 1:
                    j = i - 1
                    s.wait_ge(v1d.h, j + 1)
                    t2d.inc(s.activation(gs2[j % 2][:, :], gs2[j % 2][:, :],
                                         AF.Tanh, scale=GK), 1)

        @blk.vector
        def _(v):
            for i in range(257):
                if i < 256:
                    v.wait_ge(g1d.h, i + 1)
                    v.tensor_scalar(gs2[i % 2][:, :], gs1[i % 2][:, :],
                                    GA, 1.0, ALU.mult, ALU.add)
                    v1d.inc(v.tensor_tensor(gs2[i % 2][:, :],
                                            gs2[i % 2][:, :],
                                            ps_mm[i % 2][:, :], ALU.mult), 1)
                if i >= 1:
                    j = i - 1
                    v.wait_ge(t2d.h, j + 1)
                    if j >= 2:
                        v.wait_ge(st.h, 16 * (j - 1))
                    v.tensor_scalar(gs3[j % 2][:, :], gs2[j % 2][:, :],
                                    1.0, 0.5, ALU.add, ALU.mult)
                    gv.inc(v.tensor_tensor(gout[j % 2][:, :],
                                           gs3[j % 2][:, :],
                                           ps_mm[j % 2][:, :], ALU.mult), 1)

        @blk.sync
        def _(sy):
            for idx in range(256):
                mf, tg = idx // 4, idx % 4
                sy.wait_ge(gv.h, idx + 1)
                st.inc(sy.dma_start(
                    gt[mf * 128:mf * 128 + 128, tg * 512:tg * 512 + 512],
                    gout[idx % 2][:, :]), 16)
            sy.wait_ge(st.h, st.n)
    return p


def build_d3_launch():
    p = Prog()
    NT = 512
    NG = TOK // NT
    gt = p.din("gt", [8192, TOK], BF16)
    x2i = p.din("x2i", [D, TOK], F32)
    xo = p.din("xo", [D, TOK], F32)
    ff2P = p.din("ff2P", [128, 16 * 8192], F32)
    lgr = p.din("lgr", [3, TOK], F32)
    tT_in = p.din("tT", [32, TOK], BF16)
    wu_p = p.din("wu_p", [32, 16 * 128], F32)
    sb_in = p.din("sb_t", [128, 16], F32)
    y = p.dout("y", [D, TOK], F32)

    wu_sb = p.sb("wu_sb", [32, 16 * 128], BF16)
    tts = p.sb("tts", [32, TOK], BF16)
    m0b = p.sb("m0b", [128, TOK], BF16)
    m1b = p.sb("m1b", [128, TOK], BF16)
    m2b = p.sb("m2b", [128, TOK], BF16)
    sbl = p.sb("sbl", [128, 16], F32)
    ones_f32 = p.sb("ones_f32", [128, 1], F32)
    ones_row = p.sb("ones_row", [1, 128], F32)
    eps_t = p.sb("eps_t", [128, 1], F32)
    two_t = p.sb("two_t", [128, 1], F32)
    ps_mm = [p.psum("ps_mm", [128, 512]) for _ in range(2)]
    ps_u = p.psum("ps_u", [128, 512])
    ps_stat = p.psum("ps_stat", [33, 512])
    ps_bc = [p.psum("ps_bc", [128, 512]) for _ in range(2)]

    emit_consts(p, [(wu_sb[:, :], wu_p, True), (tts[:, :], tT_in, False),
                    (sbl[:, :], sb_in, False)],
                [(ones_f32[:, 0:1], 1.0), (ones_row[0:1, :], 1.0),
                 (eps_t[:, 0:1], 1e-5), (two_t[:, 0:1], 2.0)])

    row_stack = ExitStack()
    l0r = p.sb("l0r", [1, TOK], F32, row_stack)
    l1r = p.sb("l1r", [1, TOK], F32, row_stack)
    l2r = p.sb("l2r", [1, TOK], F32, row_stack)
    cr = [p.sb("cr", [1, TOK], F32, row_stack) for _ in range(2)]
    m0r = p.sb("m0r", [1, TOK], F32, row_stack)
    m1r = p.sb("m1r", [1, TOK], F32, row_stack)
    m2r = p.sb("m2r", [1, TOK], F32, row_stack)
    emit_consts(p, [(l0r[0:1, :], lgr[0:1, :], False),
                    (l1r[0:1, :], lgr[1:2, :], False),
                    (l2r[0:1, :], lgr[2:3, :], False)])
    cm = p.sem("d3_cm")
    bcs = p.sem("d3_bc")
    cps = p.sem("d3_cp")
    with p.block() as blk:
        @blk.vector
        def _(v):
            v.tensor_tensor(cr[0][0:1, :], l0r[0:1, :], l1r[0:1, :], ALU.is_ge)
            v.tensor_tensor(cr[1][0:1, :], l0r[0:1, :], l2r[0:1, :], ALU.is_ge)
            v.tensor_tensor(m0r[0:1, :], cr[0][0:1, :], cr[1][0:1, :], ALU.mult)
            v.tensor_tensor(cr[0][0:1, :], l1r[0:1, :], l0r[0:1, :], ALU.is_gt)
            v.tensor_tensor(cr[1][0:1, :], l1r[0:1, :], l2r[0:1, :], ALU.is_ge)
            v.tensor_tensor(m1r[0:1, :], cr[0][0:1, :], cr[1][0:1, :], ALU.mult)
            v.tensor_tensor(cr[0][0:1, :], l2r[0:1, :], l0r[0:1, :], ALU.is_gt)
            v.tensor_tensor(cr[1][0:1, :], l2r[0:1, :], l1r[0:1, :], ALU.is_gt)
            ins = v.tensor_tensor(m2r[0:1, :], cr[0][0:1, :], cr[1][0:1, :],
                                  ALU.mult)
            cm.inc(ins, 1)

        @blk.tensor
        def _(t):
            t.wait_ge(cm.h, cm.n)
            cp_base = cps.n
            for tg in range(4):
                if tg >= 1:
                    t.wait_ge(cps.h, cp_base + tg)
                sl = slice(tg * 512, tg * 512 + 512)
                t.matmul(ps_mm[0][:, :], ones_row[0:1, :], m0r[0:1, sl],
                         start=True, stop=True)
                t.matmul(ps_mm[1][:, :], ones_row[0:1, :], m1r[0:1, sl],
                         start=True, stop=True)
                ins = t.matmul(ps_bc[0][:, :], ones_row[0:1, :], m2r[0:1, sl],
                               start=True, stop=True)
                bcs.inc(ins, 1)

        @blk.vector
        def _(v):
            bc_base = bcs.n - 4
            for tg in range(4):
                v.wait_ge(bcs.h, bc_base + tg + 1)
                sl = slice(tg * 512, tg * 512 + 512)
                v.tensor_copy(m0b[:, sl], ps_mm[0][:, :])
                v.tensor_copy(m1b[:, sl], ps_mm[1][:, :])
                ins = v.tensor_copy(m2b[:, sl], ps_bc[0][:, :])
                cps.inc(ins, 1)

    row_stack.close()

    gtb = [p.sb("gtb", [128, NT], BF16) for _ in range(64)]
    w2b = [p.sb("w2b", [128, 64 * 128], BF16) for _ in range(2)]
    hf = [p.sb("hf", [128, NT], F32) for _ in range(2)]
    x2f = [p.sb("x2f", [128, NT], F32) for _ in range(2)]
    sti = [p.sb("sti", [128, NT], F32) for _ in range(NKC)]
    bl1 = p.sb("bl1", [128, NT], F32)
    bl2 = p.sb("bl2", [128, NT], F32)
    sqb = [p.sb("ln_sq", [128, NT], F32) for _ in range(2)]
    rows = {k: p.sb("ln_rows", [33, NT], F32)
            for k in ("mrow", "vrow", "m2", "rr")}
    outb = [p.sb("outb", [128, NT], F32) for _ in range(2)]

    for g_i in range(NG):
        sems = {"ld": p.sem("d3_ld"), "wl": p.sem("d3_wl"),
                "mm": p.sem("d3_mm"), "ev": p.sem("d3_ev"),
                "st": p.sem("d3_st"), "xl": p.sem("d3_xl")}
        c0 = g_i * NT
        with p.block() as blk:
            @blk.sync
            def _(sy, c0=c0, sems=sems):
                for k in range(64):
                    sems["ld"].inc(sy.dma_start(
                        gtb[k][:, :], gt[k * 128:k * 128 + 128, c0:c0 + NT]),
                        16)
                xl_base = sems["xl"].n
                ev_base = sems["ev"].n
                for mi in range(16):
                    if mi >= 2:
                        sy.wait_ge(sems["ev"].h, ev_base + mi - 1)
                    sems["xl"].inc(sy.dma_start(
                        hf[mi % 2][:, :],
                        xo[mi * 128:mi * 128 + 128, c0:c0 + NT]), 16)
                    sems["xl"].inc(sy.dma_start(
                        x2f[mi % 2][:, :],
                        x2i[mi * 128:mi * 128 + 128, c0:c0 + NT]), 16)

            @blk.gpsimd
            def _(g, c0=c0, sems=sems):
                mm_base = sems["mm"].n
                for mi in range(16):
                    if mi >= 2:
                        g.wait_ge(sems["mm"].h, mm_base + mi - 1)
                    sems["wl"].inc(g.dma_start(
                        w2b[mi % 2][:, :],
                        ff2P[:, mi * 8192:(mi + 1) * 8192]), 16)

            @blk.tensor
            def _(t, c0=c0, sems=sems):
                t.wait_ge(sems["ld"].h, sems["ld"].n)
                ev_base = sems["ev"].n
                wl_base = sems["wl"].n - 16 * 16
                for mi in range(16):
                    t.wait_ge(sems["wl"].h, wl_base + 16 * (mi + 1))
                    ps = ps_mm[mi % 2]
                    if mi >= 1:
                        t.wait_ge(sems["ev"].h, ev_base + mi)
                    for k in range(64):
                        t.matmul(ps[:, 0:NT],
                                 w2b[mi % 2][:, k * 128:k * 128 + 128],
                                 gtb[k][:, :],
                                 start=(k == 0), stop=(k == 63))
                    ins = t.matmul(ps_u[:, 0:NT],
                                   wu_sb[:, mi * 128:mi * 128 + 128],
                                   tts[0:32, c0:c0 + NT],
                                   start=True, stop=True)
                    sems["mm"].inc(ins, 1)

            @blk.vector
            def _(v, c0=c0, sems=sems):
                mm_base = sems["mm"].n - 16
                xl_base = sems["xl"].n - 32 * 16
                for mi in range(16):
                    v.wait_ge(sems["mm"].h, mm_base + mi + 1)
                    v.wait_ge(sems["xl"].h, xl_base + 32 * (mi + 1))
                    ps = ps_mm[mi % 2]
                    sl = slice(c0, c0 + NT)
                    v.tensor_tensor(bl1[:, :], ps[:, 0:NT], x2f[mi % 2][:, :],
                                    ALU.add)
                    v.tensor_tensor(bl1[:, :], bl1[:, :], hf[mi % 2][:, :],
                                    ALU.add)
                    v.tensor_tensor(sti[mi][:, :], bl1[:, :], m2b[:, sl],
                                    ALU.mult)
                    v.scalar_tensor_tensor(bl2[:, :], hf[mi % 2][:, :],
                                           two_t[:, 0:1], ps_u[:, 0:NT],
                                           ALU.mult, ALU.add)
                    v.tensor_tensor(bl2[:, :], bl2[:, :], m1b[:, sl], ALU.mult)
                    v.tensor_tensor(sti[mi][:, :], sti[mi][:, :], bl2[:, :],
                                    ALU.add)
                    v.tensor_scalar(bl2[:, :], hf[mi % 2][:, :], 2.0,
                                    sbl[:, mi:mi + 1], ALU.mult, ALU.add)
                    v.tensor_tensor(bl2[:, :], bl2[:, :], m0b[:, sl], ALU.mult)
                    ins = v.tensor_tensor(sti[mi][:, :], sti[mi][:, :],
                                          bl2[:, :], ALU.add)
                    sems["ev"].inc(ins, 1)

            st_b = sems["st"].n

            def writer(v, m, tmp, sems=sems, st_b=st_b):
                if m >= 2:
                    v.wait_ge(sems["st"].h, st_b + 16 * (m - 1))
                sems["ev"].inc(v.tensor_copy(outb[m % 2][:, :], tmp), 1)

            emit_ln(p, blk, s_tiles=[sti[k][:, :] for k in range(NKC)],
                    sq_buf=sqb, rows=rows, ones_f32=ones_f32,
                    ones_row=ones_row, eps_tile=eps_t, ps_stat=ps_stat,
                    ps_bc0=ps_bc[0], ps_bc1=ps_bc[1],
                    base_wait=(sems["ev"], sems["ev"].n), writer=writer,
                    ntok=NT)

            @blk.sync
            def _(sy, c0=c0, sems=sems):
                ev_base = sems["ev"].n - 16
                for m in range(16):
                    sy.wait_ge(sems["ev"].h, ev_base + m + 1)
                    sems["st"].inc(sy.dma_start(
                        y[m * 128:m * 128 + 128, c0:c0 + NT],
                        outb[m % 2][:, :]), 16)
                sy.wait_ge(sems["st"].h, sems["st"].n)
    return p



_PROG_CACHE = {}


def _get_prog(key, builder):
    if key not in _PROG_CACHE:
        _PROG_CACHE[key] = builder()
    return _PROG_CACHE[key]


def _run(p, in_maps):
    res = run_bass_kernel_spmd(p.nc, in_maps, core_ids=list(range(8)))
    return res.results


def kernel(**inp):
    f32 = np.float32
    xT = [np.ascontiguousarray(np.asarray(inp["L0_attn_x"][b], f32).T)
          for b in range(B)]

    pA = _get_prog("A", lambda: build_attn_launch("l0a"))
    wq = np.asarray(inp["L0_attn_wq"], f32)
    wk = np.asarray(inp["L0_attn_wk"], f32)
    wv = np.asarray(inp["L0_attn_wv"], f32)
    bq = np.asarray(inp["L0_attn_bq"], f32)
    bk = np.asarray(inp["L0_attn_bk"], f32)
    bv = np.asarray(inp["L0_attn_bv"], f32)
    in_maps = []
    for c in range(8):
        b, lane = c // 4, c % 4
        hs = slice(lane * 512, lane * 512 + 512)
        ks = slice(lane * 128, lane * 128 + 128)
        qcos, qsin = _rope_cs([lane * 4 + j for j in range(4)])
        kcos, ksin = _rope_cs([lane])
        in_maps.append({
            "xt": xT[b],
            "wq_p": _pack_lhst(wq.T[:, hs]),
            "wk_p": _pack_lhst(wk.T[:, ks]),
            "wv_p": _pack_lhst(wv.T[:, ks]),
            "bq_t": _col_table(bq[hs], 4),
            "bk_t": _col_table(bk[ks], 1),
            "bvb": np.ascontiguousarray(
                np.broadcast_to(bv[ks][None, :], (128, 128)).astype(f32)),
            "rq_cos": qcos, "rq_sin": qsin,
            "rk_cos": kcos, "rk_sin": ksin,
            "eye": _EYE, "maske": _MASK_E, "mask01": _MASK01,
        })
    resA = _run(pA, in_maps)
    ctxa = [np.concatenate([resA[b * 4 + l]["ctxT"] for l in range(4)], 0)
            for b in range(B)]

    pB = _get_prog("PJ", lambda: build_projln_launch("proj"))
    wo_p = _pack_lhst(np.asarray(inp["L0_attn_wo"], f32).T)
    bo_t = _col_table(np.asarray(inp["L0_attn_bo"], f32), 16)
    in_maps = []
    for c in range(8):
        b, lane = c // 4, c % 4
        own = slice(lane * TOK, lane * TOK + TOK)
        in_maps.append({
            "xo": np.ascontiguousarray(xT[b][:, own]),
            "cin": np.ascontiguousarray(ctxa[b][:, own]),
            "w_p": wo_p, "b_t": bo_t,
        })
    resB = _run(pB, in_maps)
    x1T = [np.concatenate([resB[b * 4 + l]["y"] for l in range(4)], 1)
           for b in range(B)]

    pC = _get_prog("C", lambda: build_attn_launch("inner"))
    wqi = np.asarray(inp["L0_ffn_wq"], f32)
    wki = np.asarray(inp["L0_ffn_wk"], f32)
    wvi = np.asarray(inp["L0_ffn_wv"], f32)
    g1_t = _col_table(np.asarray(inp["L0_ffn_ln1_scale"], f32), 16)
    b1_t = _col_table(np.asarray(inp["L0_ffn_ln1_bias"], f32), 16)
    in_maps = []
    for c in range(8):
        b, lane = c // 4, c % 4
        hs = slice(lane * 512, lane * 512 + 512)
        in_maps.append({
            "xt": x1T[b],
            "wq_p": _pack_lhst(wqi.T[:, hs]),
            "wk_p": _pack_lhst(wki.T[:, hs]),
            "wv_p": _pack_lhst(wvi.T[:, hs]),
            "g1_t": g1_t, "b1_t": b1_t,
            "eye": _EYE, "maske": _MASK_E, "mask01": _MASK01,
        })
    resC = _run(pC, in_maps)
    ctxi = [np.concatenate([resC[b * 4 + l]["ctxT"] for l in range(4)], 0)
            for b in range(B)]

    pD1 = _get_prog("D1", lambda: build_d1_launch())
    woi_p = _pack_lhst(np.asarray(inp["L0_ffn_wo"], f32).T)
    g2_t = _col_table(np.asarray(inp["L0_ffn_ln2_scale"], f32), 16)
    b2_t = _col_table(np.asarray(inp["L0_ffn_ln2_bias"], f32), 16)
    wr_p = _pack_lhst(np.asarray(inp["L0_ffn_w_router"], f32).T)
    wd_p = _pack_lhst(np.asarray(inp["L0_ffn_w_down"], f32).T)
    in_maps = []
    for c in range(8):
        b, lane = c // 4, c % 4
        own = slice(lane * TOK, lane * TOK + TOK)
        in_maps.append({
            "xo": np.ascontiguousarray(x1T[b][:, own]),
            "cin": np.ascontiguousarray(ctxi[b][:, own]),
            "wo_p": woi_p, "g2_t": g2_t, "b2_t": b2_t,
            "wr_p": wr_p, "wd_p": wd_p,
        })
    resD1 = _run(pD1, in_maps)

    pD2 = _get_prog("D2", lambda: build_d2_launch())
    ff1T_h = np.ascontiguousarray(np.asarray(inp["L0_ffn_w_ff1"], f32).T)
    in_maps = [{"xn2": resD1[c]["xn2"], "ff1T": ff1T_h} for c in range(8)]
    resD2 = _run(pD2, in_maps)

    pD3 = _get_prog("D3", lambda: build_d3_launch())
    ff2P_h = np.ascontiguousarray(
        np.asarray(inp["L0_ffn_w_ff2"], f32).T.reshape(64, 128, 16, 128)
        .transpose(1, 2, 0, 3).reshape(128, 16 * 8192))
    wu_p = np.ascontiguousarray(
        np.asarray(inp["L0_ffn_w_up"], f32).T.reshape(32, 16 * 128))
    sb_t = _col_table(np.asarray(inp["L0_ffn_simple_bias"], f32), 16)
    in_maps = []
    for c in range(8):
        b, lane = c // 4, c % 4
        own = slice(lane * TOK, lane * TOK + TOK)
        in_maps.append({
            "gt": resD2[c]["gt"], "x2i": resD1[c]["x2i"],
            "xo": np.ascontiguousarray(x1T[b][:, own]),
            "ff2P": ff2P_h, "lgr": resD1[c]["lgr"], "tT": resD1[c]["tT"],
            "wu_p": wu_p, "sb_t": sb_t,
        })
    resD3 = _run(pD3, in_maps)
    x2T = [np.concatenate([resD3[b * 4 + l]["y"] for l in range(4)], 1)
           for b in range(B)]

    pE = _get_prog("E", lambda: build_attn_launch("l1"))
    Wq = np.asarray(inp["L1_attn_Wq"], f32)
    Wk = np.asarray(inp["L1_attn_Wk"], f32)
    Wv = np.asarray(inp["L1_attn_Wv"], f32)
    bq1 = np.asarray(inp["L1_attn_bq"], f32)
    bk1 = np.asarray(inp["L1_attn_bk"], f32)
    bv1 = np.asarray(inp["L1_attn_bv"], f32)
    rp_cos, rp_sin = _rope_cs_swapped(np.arange(S))
    in_maps = []
    for c in range(8):
        b, lane = c // 4, c % 4
        hs = slice(lane * 512, lane * 512 + 512)
        in_maps.append({
            "xt": x2T[b],
            "wq_p": _pack_lhst(Wq.T[:, hs]),
            "wk_p": _pack_lhst(Wk.T[:, hs]),
            "wv_p": _pack_lhst(Wv.T[:, hs]),
            "bq_t": _col_table(bq1[hs], 4),
            "bk_t": _col_table(bk1[hs], 4),
            "bvb": np.ascontiguousarray(
                np.broadcast_to(bv1[hs][None, :], (128, 512)).astype(f32)),
            "rp_cos": rp_cos, "rp_sin": rp_sin,
            "eye": _EYE, "maske": _MASK_E, "mask01": _MASK01,
        })
    resE = _run(pE, in_maps)
    ctxl = [np.concatenate([resE[b * 4 + l]["ctxT"] for l in range(4)], 0)
            for b in range(B)]

    wo1_p = _pack_lhst(np.asarray(inp["L1_attn_Wo"], f32).T)
    bo1_t = _col_table(np.asarray(inp["L1_attn_bo"], f32), 16)
    in_maps = []
    for c in range(8):
        b, lane = c // 4, c % 4
        own = slice(lane * TOK, lane * TOK + TOK)
        in_maps.append({
            "xo": np.ascontiguousarray(x2T[b][:, own]),
            "cin": np.ascontiguousarray(ctxl[b][:, own]),
            "w_p": wo1_p, "b_t": bo1_t,
        })
    resF = _run(pB, in_maps)
    x3T = [np.concatenate([resF[b * 4 + l]["y"] for l in range(4)], 1)
           for b in range(B)]

    pG = _get_prog("G", lambda: build_projln_launch("ffn2"))
    wt_p = _pack_lhst(np.asarray(inp["L1_ffn_weight"], f32).T)
    s_t = _col_table(np.asarray(inp["L1_ffn_scale_weight"], f32), 16)
    bL_t = _col_table(np.asarray(inp["L1_ffn_bias"], f32), 16)
    in_maps = []
    for c in range(8):
        b, lane = c // 4, c % 4
        own = slice(lane * TOK, lane * TOK + TOK)
        in_maps.append({
            "xo": np.ascontiguousarray(x3T[b][:, own]),
            "w_p": wt_p, "s_t": s_t, "b_t": bL_t,
        })
    resG = _run(pG, in_maps)

    out = np.empty((B, S, D), np.float32)
    for c in range(8):
        b, lane = c // 4, c % 4
        out[b, lane * TOK:(lane + 1) * TOK, :] = resG[c]["y"].T
    return out


# version 04
# speedup vs baseline: 1.0420x; 1.0420x over previous
import sys
sys.path.insert(0, "/opt/trn_rl_repo")
import numpy as np
import ml_dtypes
from contextlib import ExitStack

import concourse.bass as bass
import concourse.mybir as mybir
from concourse.bass_utils import run_bass_kernel_spmd

F32 = mybir.dt.float32
BF16 = mybir.dt.bfloat16
AF = mybir.ActivationFunctionType
ALU = mybir.AluOpType
BF = ml_dtypes.bfloat16

B, S, D = 2, 8192, 2048
TOK = 2048
NKC = 16
SCALE = 1.0 / float(np.sqrt(128.0))
NEGC = -8.0
E1 = float(np.exp(1.0))
GK = 0.7978845608028654
GA = 0.044715


def _rope_cs(positions):
    inv = 1.0 / 10000.0 ** (np.arange(0, 128, 2, dtype=np.float64) / 128.0)
    ang = np.outer(inv, np.asarray(positions, np.float64))
    cos = np.concatenate([np.cos(ang), np.cos(ang)], 0).astype(np.float32)
    sin = np.concatenate([-np.sin(ang), np.sin(ang)], 0).astype(np.float32)
    return np.ascontiguousarray(cos), np.ascontiguousarray(sin)


def _rope_cs_swapped(positions):
    inv = 1.0 / 10000.0 ** (np.arange(0, 128, 2, dtype=np.float64) / 128.0)
    ang = np.outer(inv, np.asarray(positions, np.float64))
    cos = np.concatenate([np.cos(ang), np.cos(ang)], 0).astype(np.float32)
    sin = np.concatenate([np.sin(ang), -np.sin(ang)], 0).astype(np.float32)
    return np.ascontiguousarray(cos), np.ascontiguousarray(sin)


def _col_table(vec, nchunk):
    return np.ascontiguousarray(
        np.asarray(vec, np.float32).reshape(nchunk, 128).T)


def _pack_lhst(wT):
    Kin, M = wT.shape
    return np.ascontiguousarray(
        np.asarray(wT, np.float32).reshape(Kin // 128, 128, M)
        .transpose(1, 0, 2).reshape(128, -1))


_MASK_E = np.where(np.triu(np.ones((128, 128), bool)), np.float32(E1),
                   np.float32(1.0))
_MASK01 = np.triu(np.ones((128, 128), np.float32))
_EYE = np.eye(128, dtype=np.float32)


class Sem:
    __slots__ = ("h", "n")
    def __init__(self, h):
        self.h = h
        self.n = 0
    def inc(self, ins, k=1):
        ins.then_inc(self.h, k)
        self.n += k
        return ins


class Prog:
    def __init__(self):
        self.nc = bass.Bass("TRN2", target_bir_lowering=False)
        self.ctx = ExitStack()
        self._names = {}
        self._sems = {}

    def _nm(self, base):
        i = self._names.get(base, 0)
        self._names[base] = i + 1
        return base if i == 0 else f"{base}_{i}"

    def sb(self, base, shape, dt, stack=None):
        return (stack or self.ctx).enter_context(
            self.nc.sbuf_tensor(self._nm(base), shape, dt))

    def psum(self, base, shape, dt=F32, stack=None):
        return (stack or self.ctx).enter_context(
            self.nc.psum_tensor(self._nm(base), shape, dt))

    def sem(self, base, stack=None):
        if base not in self._sems:
            self._sems[base] = Sem(self.ctx.enter_context(
                self.nc.semaphore(base)))
        return self._sems[base]

    def din(self, name, shape, dt):
        return self.nc.dram_tensor(name, shape, dt, kind="ExternalInput").ap()

    def dout(self, name, shape, dt):
        return self.nc.dram_tensor(name, shape, dt, kind="ExternalOutput").ap()

    def dtmp(self, name, shape, dt):
        return self.nc.dram_tensor(name, shape, dt, kind="Internal").ap()

    def block(self):
        return self.nc.Block()


def emit_consts(p, loads, memsets=()):
    ld = p.sem("c_ld")
    with p.block() as blk:
        @blk.sync
        def _(sy):
            for dst, src, cast in loads:
                if not cast:
                    ld.inc(sy.dma_start(dst, src), 16)
        @blk.gpsimd
        def _(g):
            for dst, src, cast in loads:
                if cast:
                    ld.inc(g.dma_start(dst, src), 16)
            for ap, val in memsets:
                g.memset(ap, val)
        @blk.sync
        def _(sy):
            sy.wait_ge(ld.h, ld.n)
        @blk.gpsimd
        def _(g):
            g.wait_ge(ld.h, ld.n)


def emit_ln(p, blk, *, s_tiles, sq_buf, rows, ones_f32, ones_row, eps_tile,
            ps_stat, ps_bc0, ps_bc1, base_wait, writer,
            gamma_col=None, beta_col=None, ntok=512):
    nch = len(s_tiles)
    sq = p.sem("ln_sq")
    st = p.sem("ln_st")
    row = p.sem("ln_row")
    bc = p.sem("ln_bc")
    st_base = st.n

    @blk.scalar
    def _(s):
        s.wait_ge(base_wait[0].h, base_wait[1])
        for m in range(nch):
            if m >= 2:
                s.wait_ge(st.h, st_base + m - 1)
            sq.inc(s.square(sq_buf[m % 2][:, 0:ntok], s_tiles[m]), 1)

    @blk.tensor
    def _(t):
        sq_base = sq.n - nch
        t.wait_ge(base_wait[0].h, base_wait[1])
        for m in range(nch):
            t.matmul(ps_stat[0:1, 0:ntok], ones_f32[:, 0:1], s_tiles[m],
                     start=(m == 0), stop=(m == nch - 1))
        for m in range(nch):
            t.wait_ge(sq.h, sq_base + m + 1)
            ins = t.matmul(ps_stat[32:33, 0:ntok], ones_f32[:, 0:1],
                           sq_buf[m % 2][:, 0:ntok],
                           start=(m == 0), stop=(m == nch - 1))
            st.inc(ins, 1)

    @blk.vector
    def _(v):
        v.wait_ge(st.h, st.n)
        v.tensor_scalar(rows["mrow"][0:1, 0:ntok], ps_stat[0:1, 0:ntok],
                        1.0 / D, None, ALU.mult)
        ins = v.tensor_scalar(rows["vrow"][0:1, 0:ntok],
                              ps_stat[32:33, 0:ntok], 1.0 / D, None, ALU.mult)
        row.inc(ins, 1)

    @blk.scalar
    def _(s):
        s.wait_ge(row.h, row.n)
        row.inc(s.square(rows["m2"][0:1, 0:ntok], rows["mrow"][0:1, 0:ntok]), 1)

    @blk.vector
    def _(v):
        v.wait_ge(row.h, row.n)
        ins = v.tensor_tensor(rows["vrow"][0:1, 0:ntok],
                              rows["vrow"][0:1, 0:ntok],
                              rows["m2"][0:1, 0:ntok], ALU.subtract)
        row.inc(ins, 1)

    @blk.scalar
    def _(s):
        s.wait_ge(row.h, row.n)
        s.activation(rows["rr"][0:1, 0:ntok], rows["vrow"][0:1, 0:ntok],
                     AF.Ln, bias=eps_tile[0:1, 0:1], scale=1.0)
        ins = s.activation(rows["rr"][0:1, 0:ntok], rows["rr"][0:1, 0:ntok],
                           AF.Exp, scale=-0.5)
        row.inc(ins, 1)

    @blk.tensor
    def _(t):
        t.wait_ge(row.h, row.n)
        t.matmul(ps_bc0[:, 0:ntok], ones_row[0:1, :],
                 rows["mrow"][0:1, 0:ntok], start=True, stop=True)
        ins = t.matmul(ps_bc1[:, 0:ntok], ones_row[0:1, :],
                       rows["rr"][0:1, 0:ntok], start=True, stop=True)
        bc.inc(ins, 1)

    @blk.vector
    def _(v):
        v.wait_ge(bc.h, bc.n)
        for m in range(nch):
            tmp = sq_buf[m % 2][:, 0:ntok]
            v.tensor_tensor(tmp, s_tiles[m], ps_bc0[:, 0:ntok], ALU.subtract)
            v.tensor_tensor(tmp, tmp, ps_bc1[:, 0:ntok], ALU.mult)
            if gamma_col is not None:
                v.tensor_scalar(tmp, tmp, gamma_col(m), beta_col(m),
                                ALU.mult, ALU.add)
            writer(v, m, tmp)



def alloc_attn_scratch(p, ps_sc):
    at = {"ps_sc": ps_sc}
    at["qbuf"] = p.sb("at_qbuf", [128, 512], BF16)
    at["expS"] = [p.sb("at_expS", [128, 1024], BF16) for _ in range(2)]
    at["fina"] = p.sb("at_fina", [128, 129], F32)
    at["finb"] = p.sb("at_finb", [128, 129], F32)
    at["dvec"] = p.sb("at_dvec", [128, 1], F32)
    at["cbf"] = [p.sb("at_cbf", [128, 128], BF16) for _ in range(2)]
    at["ctxq"] = p.sb("at_ctxq", [128, 512], BF16)
    at["negc"] = p.sb("at_negc", [128, 1], F32)
    at["e1"] = p.sb("at_e1", [128, 1], F32)
    at["eye"] = p.sb("at_eye", [128, 128], BF16)
    at["mask_e"] = p.sb("at_maske", [128, 128], BF16)
    at["mask_01"] = p.sb("at_mask01", [128, 128], BF16)
    at["ps_ctx"] = [p.psum("ps_ctx", [128, 512]) for _ in range(3)]
    return at


def emit_attention(p, *, kind, qT_dram, k_res, v_res, ctx_out, at):
    shared_kv = (kind == "l0a")

    def ctx_tile(i):
        return at["ps_ctx"][i // 3][:, (i % 3) * 129:(i % 3) * 129 + 129]

    for pr in range(4):
        kvb = 0 if shared_kv else pr * S
        vvb = 0 if shared_kv else pr * 64 * 130
        for qc in range(16):
            kbs = (list(range(qc * 4 + 4)) if kind == "causal"
                   else list(range(64)))
            n = len(kbs)
            j0_of = {kb: (max(0, kb - qc * 4) if kind == "causal" else 0)
                     for kb in kbs}
            diag_of = {kb: (kb - qc * 4
                            if (kind in ("l0a", "causal")
                                and 0 <= kb - qc * 4 < 4) else None)
                       for kb in kbs}
            diag_kbs = [kb for kb in kbs if diag_of[kb] is not None]
            pv_sched = {j: [] for j in range(4)}
            for kb in kbs:
                for j in range(j0_of[kb], 4):
                    if kind == "causal" and kb > qc * 4 + j:
                        continue
                    cls = 1 if (kind == "l0a" and kb < qc * 4 + j) else 0
                    pv_sched[j].append((kb, cls))

            ld = p.sem("a_ld")
            scm = p.sem("a_sc")
            expd = p.sem("a_exp")
            vmm = p.sem("a_vmm")
            pvd = p.sem("a_pv")
            fin = p.sem("a_fin")
            trs = p.sem("a_tr")
            cpd = p.sem("a_cp")
            sto = p.sem("a_st")
            ctxz = p.sem("a_cz")
            rcp = p.sem("a_rc")
            qbuf, expS = at["qbuf"], at["expS"]

            with p.block() as blk:
                @blk.sync
                def _(sy):
                    ld.inc(sy.dma_start(
                        qbuf[:, :], qT_dram[pr * 128:pr * 128 + 128,
                                            qc * 512:qc * 512 + 512]), 16)

                @blk.tensor
                def _(t):
                    t.wait_ge(ld.h, ld.n)
                    expd_base = expd.n
                    vmm_base = vmm.n
                    ctxz_base = ctxz.n

                    def score(i):
                        kb = kbs[i]
                        j0 = j0_of[kb]
                        base = (i % 2) * 512
                        ins = t.matmul(
                            at["ps_sc"][(i // 2) % 2][:, base + j0 * 128:
                                                      base + 512],
                            k_res[:, kvb + kb * 128:kvb + kb * 128 + 128],
                            qbuf[:, j0 * 128:512], start=True, stop=True)
                        scm.inc(ins, 1)

                    def pv(i):
                        kb = kbs[i]
                        base = (i % 2) * 512
                        ins = None
                        for j in range(j0_of[kb], 4):
                            if kind == "causal" and kb > qc * 4 + j:
                                continue
                            cls = 1 if (kind == "l0a" and kb < qc * 4 + j) else 0
                            ins = t.matmul(
                                ctx_tile(j * 2 + cls),
                                expS[(i // 2) % 2][:, base + j * 128:
                                                   base + j * 128 + 128],
                                v_res[:, vvb + kb * 130:vvb + kb * 130 + 129],
                                start=False, stop=False,
                                skip_group_check=True)
                        pvd.inc(ins, 1)

                    for i in range(min(4, n)):
                        score(i)
                    ndiag_seen = 0
                    for g in range(n // 2):
                        i0, i1 = 2 * g, 2 * g + 1
                        t.wait_ge(expd.h, expd_base + g + 1)
                        if g == 0:
                            t.wait_ge(ctxz.h, ctxz_base + 1)
                        nd = sum(1 for i in (i0, i1)
                                 if diag_of[kbs[i]] is not None)
                        if nd:
                            ndiag_seen += nd
                            t.wait_ge(vmm.h, vmm_base + ndiag_seen)
                        pv(i0)
                        pv(i1)
                        if i1 + 4 < n:
                            score(i1 + 3)
                            score(i1 + 4)

                @blk.scalar
                def _(s):
                    scm_base = scm.n - n
                    pvd_base = pvd.n - n
                    for g in range(n // 2):
                        i0, i1 = 2 * g, 2 * g + 1
                        s.wait_ge(scm.h, scm_base + i1 + 1)
                        if g >= 2:
                            s.wait_ge(pvd.h, pvd_base + i0 - 2)
                        j0a = j0_of[kbs[i0]]
                        ins = s.activation(
                            expS[g % 2][:, j0a * 128:1024],
                            at["ps_sc"][g % 2][:, j0a * 128:1024],
                            AF.Exp, bias=at["negc"][:, 0:1], scale=SCALE)
                        expd.inc(ins, 1)

                @blk.vector
                def _(v):
                    expd_base = expd.n - n // 2
                    pvd_goal = pvd.n
                    v.memset(at["ps_ctx"][0][:, :], 0.0)
                    v.memset(at["ps_ctx"][1][:, :], 0.0)
                    ctxz.inc(v.memset(at["ps_ctx"][2][:, :], 0.0), 1)
                    mask = at["mask_e"] if kind == "l0a" else at["mask_01"]
                    for di, kb in enumerate(diag_kbs):
                        i = kbs.index(kb)
                        j = diag_of[kb]
                        v.wait_ge(expd.h, expd_base + i // 2 + 1)
                        sl = expS[(i // 2) % 2][:, (i % 2) * 512 + j * 128:
                                                (i % 2) * 512 + j * 128 + 128]
                        ins = v.tensor_tensor(sl, sl, mask[:, :], ALU.mult)
                        vmm.inc(ins, 1)
                    v.wait_ge(pvd.h, pvd_goal)
                    trs_base = trs.n

                    def fin_j(j):
                        has_b = any(c == 1 for _, c in pv_sched[j])
                        fb = at["finb"]
                        if j >= 2:
                            v.wait_ge(trs.h, trs_base + j - 1)
                        if has_b:
                            v.tensor_copy(at["fina"][:, :], ctx_tile(j * 2 + 1))
                            ins = v.scalar_tensor_tensor(
                                fb[:, :], at["fina"][:, :], at["e1"][:, 0:1],
                                ctx_tile(j * 2), ALU.mult, ALU.add)
                        else:
                            ins = v.tensor_copy(fb[:, :], ctx_tile(j * 2))
                        rcp.inc(ins, 1)
                        v.wait_ge(rcp.h, rcp.n)
                        rcp.inc(v.reciprocal(at["dvec"][:, :],
                                             fb[:, 128:129]), 1)
                        v.wait_ge(rcp.h, rcp.n)
                        ins = v.tensor_scalar(at["cbf"][j % 2][:, :],
                                              fb[:, 0:128], at["dvec"][:, 0:1],
                                              None, ALU.mult)
                        fin.inc(ins, 1)

                    def copy_j(j):
                        v.wait_ge(trs.h, trs_base + j + 1)
                        ins = v.tensor_copy(
                            at["ctxq"][:, j * 128:j * 128 + 128],
                            at["ps_tr"][:, 0:128])
                        cpd.inc(ins, 1)

                    fin_j(0)
                    fin_j(1)
                    copy_j(0)
                    fin_j(2)
                    copy_j(1)
                    fin_j(3)
                    copy_j(2)
                    copy_j(3)

                @blk.tensor
                def _(t):
                    fin_base = fin.n - 4
                    cpd_base = cpd.n - 4
                    for j in range(4):
                        t.wait_ge(fin.h, fin_base + j + 1)
                        if j >= 1:
                            t.wait_ge(cpd.h, cpd_base + j)
                        ins = t.transpose(at["ps_tr"][:, 0:128],
                                          at["cbf"][j % 2][:, :],
                                          at["eye"][:, :])
                        trs.inc(ins, 1)

                @blk.sync
                def _(sy):
                    sy.wait_ge(cpd.h, cpd.n)
                    sto.inc(sy.dma_start(
                        ctx_out[pr * 128:pr * 128 + 128,
                                qc * 512:qc * 512 + 512], at["ctxq"][:, :]), 16)
                    sy.wait_ge(sto.h, sto.n)



def emit_proj(p, *, tgs, n_k, n_m, w_tile, rhs_tile, load_rhs, evac, ps_pool,
              ntok=512):
    for tg in tgs:
        sems = {"ld": p.sem("pr_ld"), "mm": p.sem("pr_mm"),
                "ev": p.sem("pr_ev")}
        with p.block() as blk:
            @blk.sync
            def _(sy, tg=tg, sems=sems):
                load_rhs(sy, None, tg, sems)
            @blk.gpsimd
            def _(g, tg=tg, sems=sems):
                load_rhs(None, g, tg, sems)
            @blk.tensor
            def _(t, tg=tg, sems=sems):
                t.wait_ge(sems["ld"].h, sems["ld"].n)
                ev_base = sems["ev"].n
                for mi in range(n_m):
                    ps = ps_pool[mi % 2]
                    if mi >= 2:
                        t.wait_ge(sems["ev"].h, ev_base + mi - 1)
                    ins = None
                    for k in range(n_k):
                        ins = t.matmul(ps[:, 0:ntok], w_tile(k, mi),
                                       rhs_tile(k, tg),
                                       start=(k == 0), stop=(k == n_k - 1))
                    sems["mm"].inc(ins, 1)
            evac(blk, tg, sems, ps_pool)



def build_attn_launch(kind, debug=False, upto="attn"):
    _phases = ["ln1", "q", "k", "v", "attn"]
    _lim = _phases.index(upto)
    p = Prog()
    kw = 128 if kind == "l0a" else 512
    nm_kv = kw // 128
    xt = p.din("xt", [D, S], F32)
    wq_p = p.din("wq_p", [128, NKC * 512], F32)
    wk_p = p.din("wk_p", [128, NKC * kw], F32)
    wv_p = p.din("wv_p", [128, NKC * kw], F32)
    eye_in = p.din("eye", [128, 128], F32)
    maske_in = p.din("maske", [128, 128], F32)
    mask01_in = p.din("mask01", [128, 128], F32)
    if kind in ("l0a", "l1"):
        bq_in = p.din("bq_t", [128, 4], F32)
        bk_in = p.din("bk_t", [128, nm_kv], F32)
        bvb_in = p.din("bvb", [128, kw], F32)
    if kind == "l0a":
        rqc_in = p.din("rq_cos", [128, 4], F32)
        rqs_in = p.din("rq_sin", [128, 4], F32)
        rkc_in = p.din("rk_cos", [128, 1], F32)
        rks_in = p.din("rk_sin", [128, 1], F32)
    if kind == "l1":
        rpc_in = p.din("rp_cos", [128, S], F32)
        rps_in = p.din("rp_sin", [128, S], F32)
    if kind == "inner":
        g1_in = p.din("g1_t", [128, 16], F32)
        b1_in = p.din("b1_t", [128, 16], F32)
    ctx_out = p.dout("ctxT", [512, S], BF16)
    kw_ = 128 if kind == "l0a" else 512
    if debug:
        qT = p.dout("qT_s", [512, S], BF16)
        kdbg = p.dout("kdbg", [128, (kw_ // 128) * S], BF16)
        vdbg = p.dout("vdbg", [128, (kw_ // 128) * 64 * 130], BF16)
    else:
        qT = p.dtmp("qT_s", [512, S], BF16)
    if kind == "inner":
        xn1 = p.dtmp("xn1_s", [D, S], BF16)

    k_res = p.sb("k_res", [128, nm_kv * S], BF16)
    v_res = p.sb("v_res", [128, nm_kv * 64 * 130], BF16)
    ones_f32 = p.sb("ones_f32", [128, 1], F32)
    ones_row = p.sb("ones_row", [1, 128], F32)
    eps_t = p.sb("eps_t", [128, 1], F32)
    ps_w = [p.psum("ps_w", [128, 1024]) for _ in range(2)]
    ps_mm = [ps_w[0][:, 0:512], ps_w[1][:, 0:512]]
    at = alloc_attn_scratch(p, ps_mm)
    more = {}
    consts = [(at["eye"][:, :], eye_in, True),
              (at["mask_e"][:, :], maske_in, True),
              (at["mask_01"][:, :], mask01_in, True)]
    msets = [(at["negc"][:, 0:1], NEGC), (at["e1"][:, 0:1], E1),
             (ones_f32[:, 0:1], 1.0), (ones_row[0:1, :], 1.0),
             (eps_t[:, 0:1], 1e-5), (v_res[:, :], 1.0)]
    if kind in ("l0a", "l1"):
        more["bq"] = p.sb("c_bq", [128, 4], F32)
        more["bk"] = p.sb("c_bk", [128, nm_kv], F32)
        more["bvb"] = p.sb("c_bvb", [128, kw], F32)
        consts += [(more["bq"][:, :], bq_in, False),
                   (more["bk"][:, :], bk_in, False),
                   (more["bvb"][:, :], bvb_in, False)]
    if kind == "l0a":
        for nmv, src in (("rq_cos", rqc_in), ("rq_sin", rqs_in),
                         ("rk_cos", rkc_in), ("rk_sin", rks_in)):
            more[nmv] = p.sb("c_" + nmv, list(src.shape), F32)
            consts.append((more[nmv][:, :], src, False))
    if kind == "l1":
        more["rc"] = [p.sb("c_rc", [128, 512], F32) for _ in range(2)]
        more["rs"] = [p.sb("c_rs", [128, 512], F32) for _ in range(2)]
    if kind == "inner":
        more["g1"] = p.sb("c_g1", [128, 16], F32)
        more["b1"] = p.sb("c_b1", [128, 16], F32)
        consts += [(more["g1"][:, :], g1_in, False),
                   (more["b1"][:, :], b1_in, False)]
    emit_consts(p, consts, msets)

    with ExitStack() as phase_stack:
        xb = [p.sb("xb", [128, 512], BF16, phase_stack) for _ in range(NKC)]
        q0 = p.sb("q0", [128, 512], F32, phase_stack)
        rt = p.sb("rt", [128, 512], F32, phase_stack)
        qex = [p.sb("qex", [128, 512], BF16, phase_stack) for _ in range(2)]

        def load_x(sy, g, tg, sems):
            if kind != "inner" and g is not None:
                for k in range(NKC):
                    sems["ld"].inc(g.dma_start(
                        xb[k][:, :],
                        xt[k * 128:k * 128 + 128, tg * 512:tg * 512 + 512]), 16)
            if kind == "inner" and sy is not None:
                for k in range(NKC):
                    sems["ld"].inc(sy.dma_start(
                        xb[k][:, :],
                        xn1[k * 128:k * 128 + 128, tg * 512:tg * 512 + 512]), 16)

        if kind == "inner":
            with ExitStack() as st1:
                xf = [p.sb("xf", [128, 512], F32, st1) for _ in range(NKC)]
                sqb = [p.sb("ln_sq", [128, 512], F32, st1) for _ in range(2)]
                rows = {k: p.sb("ln_rows", [33, 512], F32, st1)
                        for k in ("mrow", "vrow", "m2", "rr")}
                ps_stat = p.psum("ps_stat", [33, 512], stack=st1)
                for tg in range(16):
                    ldf = p.sem("lnf_ld")
                    stf = p.sem("lnf_st")
                    with p.block() as blk:
                        @blk.sync
                        def _(sy, tg=tg, ldf=ldf):
                            for k in range(NKC):
                                ldf.inc(sy.dma_start(
                                    xf[k][:, :],
                                    xt[k * 128:k * 128 + 128,
                                       tg * 512:tg * 512 + 512]), 16)

                        def writer(v, m, tmp, stf=stf):
                            stf.inc(v.tensor_copy(xb[m][:, :], tmp), 1)

                        emit_ln(p, blk,
                                s_tiles=[xf[k][:, :] for k in range(NKC)],
                                sq_buf=sqb, rows=rows, ones_f32=ones_f32,
                                ones_row=ones_row, eps_tile=eps_t,
                                ps_stat=ps_stat, ps_bc0=at["ps_ctx"][0],
                                ps_bc1=at["ps_ctx"][1], base_wait=(ldf, ldf.n),
                                writer=writer,
                                gamma_col=lambda m: more["g1"][:, m:m + 1],
                                beta_col=lambda m: more["b1"][:, m:m + 1])

                        @blk.sync
                        def _(sy, tg=tg, stf=stf):
                            sy.wait_ge(stf.h, stf.n)
                            for k in range(NKC):
                                stf.inc(sy.dma_start(
                                    xn1[k * 128:k * 128 + 128,
                                        tg * 512:tg * 512 + 512],
                                    xb[k][:, :]), 16)
                            sy.wait_ge(stf.h, stf.n)

        if _lim < 1:
            return p
        with ExitStack() as stq_s:
            wq_sb = p.sb("wq_sb", [128, NKC * 512], BF16, stq_s)
            emit_consts(p, [(wq_sb[:, :], wq_p, True)])

            def evac_q(blk, tg, sems, pool):
                stq = p.sem("q_st")

                @blk.sync
                def _(sy, tg=tg, sems=sems):
                    if kind == "l1":
                        b = tg % 2
                        sems["ld"].inc(sy.dma_start(
                            more["rc"][b][:, :],
                            rpc_in[:, tg * 512:tg * 512 + 512]), 16)
                        sems["ld"].inc(sy.dma_start(
                            more["rs"][b][:, :],
                            rps_in[:, tg * 512:tg * 512 + 512]), 16)

                @blk.vector
                def _(v, tg=tg, sems=sems, stq=stq):
                    if kind == "l1":
                        v.wait_ge(sems["ld"].h, sems["ld"].n)
                    mm_base = sems["mm"].n - 4
                    stq_base = stq.n
                    for mi in range(4):
                        v.wait_ge(sems["mm"].h, mm_base + mi + 1)
                        if mi >= 2:
                            v.wait_ge(stq.h, stq_base + 16 * (mi - 1))
                        ps = pool[mi % 2]
                        qx = qex[mi % 2]
                        if kind == "l0a":
                            v.tensor_scalar(q0[:, :], ps[:, 0:512],
                                            more["bq"][:, mi:mi + 1], None,
                                            ALU.add)
                            v.tensor_scalar(rt[0:64, :], q0[64:128, :],
                                            more["rq_sin"][0:64, mi:mi + 1],
                                            None, ALU.mult)
                            v.tensor_scalar(rt[64:128, :], q0[0:64, :],
                                            more["rq_sin"][64:128, mi:mi + 1],
                                            None, ALU.mult)
                            ins = v.scalar_tensor_tensor(
                                qx[:, :], q0[:, :],
                                more["rq_cos"][:, mi:mi + 1], rt[:, :],
                                ALU.mult, ALU.add)
                        elif kind == "l1":
                            b = tg % 2
                            v.tensor_scalar(q0[:, :], ps[:, 0:512],
                                            more["bq"][:, mi:mi + 1], None,
                                            ALU.add)
                            v.tensor_tensor(rt[0:64, :], q0[64:128, :],
                                            more["rs"][b][64:128, :], ALU.mult)
                            v.tensor_tensor(rt[64:128, :], q0[0:64, :],
                                            more["rs"][b][0:64, :], ALU.mult)
                            v.tensor_tensor(q0[:, :], q0[:, :],
                                            more["rc"][b][:, :], ALU.mult)
                            ins = v.tensor_tensor(qx[:, :], q0[:, :],
                                                  rt[:, :], ALU.add)
                        else:
                            ins = v.tensor_copy(qx[:, :], ps[:, 0:512])
                        sems["ev"].inc(ins, 1)

                @blk.sync
                def _(sy, tg=tg, sems=sems, stq=stq):
                    ev_base = sems["ev"].n - 4
                    for mi in range(4):
                        sy.wait_ge(sems["ev"].h, ev_base + mi + 1)
                        stq.inc(sy.dma_start(
                            qT[mi * 128:mi * 128 + 128,
                               tg * 512:tg * 512 + 512], qex[mi % 2][:, :]), 16)
                    sy.wait_ge(stq.h, stq.n)

            emit_proj(p, tgs=list(range(16)), n_k=NKC, n_m=4,
                      w_tile=lambda k, mi: wq_sb[:, k * 512 + mi * 128:
                                                 k * 512 + mi * 128 + 128],
                      rhs_tile=lambda k, tg: xb[k][:, :], load_rhs=load_x,
                      evac=evac_q, ps_pool=ps_mm)

        if _lim < 2:
            return p
        with ExitStack() as stk_s:
            wk_sb = p.sb("wk_sb", [128, NKC * kw], BF16, stk_s)
            emit_consts(p, [(wk_sb[:, :], wk_p, True)])

            def evac_k(blk, tg, sems, pool):
                @blk.sync
                def _(sy, tg=tg, sems=sems):
                    if kind == "l1":
                        b = tg % 2
                        sems["ld"].inc(sy.dma_start(
                            more["rc"][b][:, :],
                            rpc_in[:, tg * 512:tg * 512 + 512]), 16)
                        sems["ld"].inc(sy.dma_start(
                            more["rs"][b][:, :],
                            rps_in[:, tg * 512:tg * 512 + 512]), 16)

                @blk.vector
                def _(v, tg=tg, sems=sems):
                    if kind == "l1":
                        v.wait_ge(sems["ld"].h, sems["ld"].n)
                    mm_base = sems["mm"].n - nm_kv
                    for mi in range(nm_kv):
                        v.wait_ge(sems["mm"].h, mm_base + mi + 1)
                        ps = pool[mi % 2]
                        dst = k_res[:, mi * S + tg * 512:
                                    mi * S + tg * 512 + 512]
                        if kind == "l0a":
                            v.tensor_scalar(q0[:, :], ps[:, 0:512],
                                            more["bk"][:, mi:mi + 1], None,
                                            ALU.add)
                            v.tensor_scalar(rt[0:64, :], q0[64:128, :],
                                            more["rk_sin"][0:64, 0:1], None,
                                            ALU.mult)
                            v.tensor_scalar(rt[64:128, :], q0[0:64, :],
                                            more["rk_sin"][64:128, 0:1], None,
                                            ALU.mult)
                            ins = v.scalar_tensor_tensor(
                                dst, q0[:, :], more["rk_cos"][:, 0:1],
                                rt[:, :], ALU.mult, ALU.add)
                        elif kind == "l1":
                            b = tg % 2
                            v.tensor_scalar(q0[:, :], ps[:, 0:512],
                                            more["bk"][:, mi:mi + 1], None,
                                            ALU.add)
                            v.tensor_tensor(rt[0:64, :], q0[64:128, :],
                                            more["rs"][b][64:128, :], ALU.mult)
                            v.tensor_tensor(rt[64:128, :], q0[0:64, :],
                                            more["rs"][b][0:64, :], ALU.mult)
                            v.tensor_tensor(q0[:, :], q0[:, :],
                                            more["rc"][b][:, :], ALU.mult)
                            ins = v.tensor_tensor(dst, q0[:, :], rt[:, :],
                                                  ALU.add)
                        else:
                            ins = v.tensor_copy(dst, ps[:, 0:512])
                        sems["ev"].inc(ins, 1)

            emit_proj(p, tgs=list(range(16)), n_k=NKC, n_m=nm_kv,
                      w_tile=lambda k, mi: wk_sb[:, k * kw + mi * 128:
                                                 k * kw + mi * 128 + 128],
                      rhs_tile=lambda k, tg: xb[k][:, :], load_rhs=load_x,
                      evac=evac_k, ps_pool=ps_mm)

        if _lim < 3:
            return p
        with ExitStack() as stv_s:
            wv_sb = p.sb("wv_sb", [128, NKC * kw], BF16, stv_s)
            emit_consts(p, [(wv_sb[:, :], wv_p, True)])
            for tg in range(16):
                sems = {"ld": p.sem("v_ld"), "mm": p.sem("v_mm"),
                        "ev": p.sem("v_ev")}
                with p.block() as blk:
                    @blk.sync
                    def _(sy, tg=tg, sems=sems):
                        load_x(sy, None, tg, sems)
                    @blk.gpsimd
                    def _(g, tg=tg, sems=sems):
                        load_x(None, g, tg, sems)
                    @blk.tensor
                    def _(t, tg=tg, sems=sems):
                        t.wait_ge(sems["ld"].h, sems["ld"].n)
                        ev_base = sems["ev"].n
                        for tt in range(4):
                            ps = ps_mm[tt % 2]
                            if tt >= 2:
                                t.wait_ge(sems["ev"].h, ev_base + tt - 1)
                            ins = None
                            for k in range(NKC):
                                ins = t.matmul(
                                    ps[:, 0:kw],
                                    xb[k][:, tt * 128:tt * 128 + 128],
                                    wv_sb[:, k * kw:(k + 1) * kw],
                                    start=(k == 0), stop=(k == NKC - 1))
                            sems["mm"].inc(ins, 1)
                    @blk.vector
                    def _(v, tg=tg, sems=sems):
                        mm_base = sems["mm"].n - 4
                        for tt in range(4):
                            v.wait_ge(sems["mm"].h, mm_base + tt + 1)
                            ps = ps_mm[tt % 2]
                            kb = tg * 4 + tt
                            ins = None
                            for j4 in range(nm_kv):
                                dst = v_res[:, j4 * 64 * 130 + kb * 130:
                                            j4 * 64 * 130 + kb * 130 + 128]
                                if kind == "inner":
                                    ins = v.tensor_copy(
                                        dst, ps[:, j4 * 128:j4 * 128 + 128])
                                else:
                                    ins = v.tensor_tensor(
                                        dst, ps[:, j4 * 128:j4 * 128 + 128],
                                        more["bvb"][:, j4 * 128:j4 * 128 + 128],
                                        ALU.add)
                            sems["ev"].inc(ins, 1)

    if _lim < 4:
        return p
    if debug:
        dsem = p.sem("dbg_st")
        with p.block() as blk:
            @blk.sync
            def _(sy):
                dsem.inc(sy.dma_start(kdbg, k_res[:, :]), 16)
                dsem.inc(sy.dma_start(vdbg, v_res[:, :]), 16)
                sy.wait_ge(dsem.h, dsem.n)

    at["ps_tr"] = p.psum("ps_tr", [128, 1024], BF16)
    at["ps_sc"] = ps_w
    emit_attention(p, kind=("causal" if kind == "inner"
                            else ("l0a" if kind == "l0a" else "full")),
                   qT_dram=qT, k_res=k_res, v_res=v_res, ctx_out=ctx_out,
                   at=at)
    return p


def build_projln_launch(kind):
    p = Prog()
    xo = p.din("xo", [D, TOK], F32)
    w_p = p.din("w_p", [128, NKC * D], F32)
    b_in = p.din("b_t", [128, 16], F32)
    if kind == "proj":
        cin = p.din("cin", [D, TOK], BF16)
    else:
        s_in = p.din("s_t", [128, 16], F32)
    y = p.dout("y", [D, TOK], F32)

    w_sb = p.sb("w_sb", [128, NKC * D], BF16)
    rhs = [p.sb("rhs", [128, 512], BF16) for _ in range(NKC)]
    xf = [p.sb("xf", [128, 512], F32) for _ in range(NKC)]
    sti = [p.sb("sti", [128, 512], F32) for _ in range(NKC)]
    sqb = [p.sb("ln_sq", [128, 512], F32) for _ in range(2)]
    rows = {k: p.sb("ln_rows", [33, 512], F32)
            for k in ("mrow", "vrow", "m2", "rr")}
    outb = [p.sb("outb", [128, 512], F32) for _ in range(2)]
    ones_f32 = p.sb("ones_f32", [128, 1], F32)
    ones_row = p.sb("ones_row", [1, 128], F32)
    eps_t = p.sb("eps_t", [128, 1], F32)
    btl = p.sb("btl", [128, 16], F32)
    stl = p.sb("stl", [128, 16], F32)
    ps_mm = [p.psum("ps_mm", [128, 512]) for _ in range(2)]
    ps_stat = p.psum("ps_stat", [33, 512])
    ps_bc = [p.psum("ps_bc", [128, 512]) for _ in range(2)]

    consts = [(w_sb[:, :], w_p, True), (btl[:, :], b_in, False)]
    if kind == "ffn2":
        consts.append((stl[:, :], s_in, False))
    emit_consts(p, consts, [(ones_f32[:, 0:1], 1.0), (ones_row[0:1, :], 1.0),
                            (eps_t[:, 0:1], 1e-5)])

    for tg in range(4):
        sems = {"ld": p.sem("pj_ld"), "mm": p.sem("pj_mm"),
                "ev": p.sem("pj_ev"), "st": p.sem("pj_st")}
        with p.block() as blk:
            @blk.sync
            def _(sy, tg=tg, sems=sems):
                for k in range(NKC):
                    sems["ld"].inc(sy.dma_start(
                        xf[k][:, :],
                        xo[k * 128:k * 128 + 128, tg * 512:tg * 512 + 512]), 16)
                if kind == "proj":
                    for k in range(NKC):
                        sems["ld"].inc(sy.dma_start(
                            rhs[k][:, :],
                            cin[k * 128:k * 128 + 128,
                                tg * 512:tg * 512 + 512]), 16)

            @blk.gpsimd
            def _(g, tg=tg, sems=sems):
                if kind == "ffn2":
                    for k in range(NKC):
                        sems["ld"].inc(g.dma_start(
                            rhs[k][:, :],
                            xo[k * 128:k * 128 + 128,
                               tg * 512:tg * 512 + 512]), 16)

            @blk.tensor
            def _(t, tg=tg, sems=sems):
                t.wait_ge(sems["ld"].h, sems["ld"].n)
                ev_base = sems["ev"].n
                for mi in range(16):
                    ps = ps_mm[mi % 2]
                    if mi >= 2:
                        t.wait_ge(sems["ev"].h, ev_base + mi - 1)
                    ins = None
                    for k in range(NKC):
                        ins = t.matmul(ps[:, :],
                                       w_sb[:, k * D + mi * 128:
                                            k * D + mi * 128 + 128],
                                       rhs[k][:, :],
                                       start=(k == 0), stop=(k == NKC - 1))
                    sems["mm"].inc(ins, 1)

            @blk.vector
            def _(v, tg=tg, sems=sems):
                mm_base = sems["mm"].n - 16
                for mi in range(16):
                    v.wait_ge(sems["mm"].h, mm_base + mi + 1)
                    ps = ps_mm[mi % 2]
                    if kind == "proj":
                        ins = v.scalar_tensor_tensor(
                            sti[mi][:, :], ps[:, :], btl[:, mi:mi + 1],
                            xf[mi][:, :], ALU.add, ALU.add)
                    else:
                        v.tensor_scalar(sti[mi][:, :], ps[:, :],
                                        stl[:, mi:mi + 1], btl[:, mi:mi + 1],
                                        ALU.mult, ALU.add)
                        ins = v.tensor_tensor(sti[mi][:, :], sti[mi][:, :],
                                              xf[mi][:, :], ALU.add)
                    sems["ev"].inc(ins, 1)

            st_b = sems["st"].n

            def writer(v, m, tmp, sems=sems, st_b=st_b):
                if m >= 2:
                    v.wait_ge(sems["st"].h, st_b + 16 * (m - 1))
                sems["ev"].inc(v.tensor_copy(outb[m % 2][:, :], tmp), 1)

            emit_ln(p, blk, s_tiles=[sti[k][:, :] for k in range(NKC)],
                    sq_buf=sqb, rows=rows, ones_f32=ones_f32,
                    ones_row=ones_row, eps_tile=eps_t, ps_stat=ps_stat,
                    ps_bc0=ps_bc[0], ps_bc1=ps_bc[1],
                    base_wait=(sems["ev"], sems["ev"].n), writer=writer)

            @blk.sync
            def _(sy, tg=tg, sems=sems):
                ev_base = sems["ev"].n - 16
                for m in range(16):
                    sy.wait_ge(sems["ev"].h, ev_base + m + 1)
                    sems["st"].inc(sy.dma_start(
                        y[m * 128:m * 128 + 128, tg * 512:tg * 512 + 512],
                        outb[m % 2][:, :]), 16)
                sy.wait_ge(sems["st"].h, sems["st"].n)
    return p


def build_d1_launch():
    p = Prog()
    cin = p.din("cin", [D, TOK], BF16)
    xo = p.din("xo", [D, TOK], F32)
    wo_p = p.din("wo_p", [128, NKC * D], F32)
    g2_in = p.din("g2_t", [128, 16], F32)
    b2_in = p.din("b2_t", [128, 16], F32)
    wr_p = p.din("wr_p", [128, NKC * 3], F32)
    wd_p = p.din("wd_p", [128, NKC * 32], F32)
    x2i = p.dout("x2i", [D, TOK], F32)
    xn2 = p.dout("xn2", [D, TOK], BF16)
    lgr_o = p.dout("lgr", [3, TOK], F32)
    tT_o = p.dout("tT", [32, TOK], BF16)

    w_sb = p.sb("w_sb", [128, NKC * D], BF16)
    wr_sb = p.sb("wr_sb", [128, NKC * 3], F32)
    wd_sb = p.sb("wd_sb", [128, NKC * 32], BF16)
    rhs = [p.sb("rhs", [128, 512], BF16) for _ in range(NKC)]
    xf = [p.sb("xf", [128, 512], F32) for _ in range(NKC)]
    sti = [p.sb("sti", [128, 512], F32) for _ in range(NKC)]
    sqb = [p.sb("ln_sq", [128, 512], F32) for _ in range(2)]
    rows = {k: p.sb("ln_rows", [33, 512], F32)
            for k in ("mrow", "vrow", "m2", "rr")}
    outb = [p.sb("outb", [128, 512], BF16) for _ in range(2)]
    lgs = p.sb("lgs", [33, 512], F32)
    tts = p.sb("tts", [32, 512], BF16)
    g2l = p.sb("g2l", [128, 16], F32)
    b2l = p.sb("b2l", [128, 16], F32)
    ones_f32 = p.sb("ones_f32", [128, 1], F32)
    ones_row = p.sb("ones_row", [1, 128], F32)
    eps_t = p.sb("eps_t", [128, 1], F32)
    ps_mm = [p.psum("ps_mm", [128, 512]) for _ in range(2)]
    ps_stat = p.psum("ps_stat", [33, 512])
    ps_bc = [p.psum("ps_bc", [128, 512]) for _ in range(2)]
    ps_r = p.psum("ps_r", [33, 512])

    emit_consts(p, [(w_sb[:, :], wo_p, True), (wr_sb[:, :], wr_p, False),
                    (wd_sb[:, :], wd_p, True), (g2l[:, :], g2_in, False),
                    (b2l[:, :], b2_in, False)],
                [(ones_f32[:, 0:1], 1.0), (ones_row[0:1, :], 1.0),
                 (eps_t[:, 0:1], 1e-5)])

    for tg in range(4):
        sems = {"ld": p.sem("d1_ld"), "mm": p.sem("d1_mm"),
                "ev": p.sem("d1_ev"), "st": p.sem("d1_st"),
                "rt": p.sem("d1_rt")}
        with p.block() as blk:
            @blk.sync
            def _(sy, tg=tg, sems=sems):
                for k in range(NKC):
                    sems["ld"].inc(sy.dma_start(
                        xf[k][:, :],
                        xo[k * 128:k * 128 + 128, tg * 512:tg * 512 + 512]), 16)
                for k in range(NKC):
                    sems["ld"].inc(sy.dma_start(
                        rhs[k][:, :],
                        cin[k * 128:k * 128 + 128, tg * 512:tg * 512 + 512]),
                        16)

            @blk.tensor
            def _(t, tg=tg, sems=sems):
                t.wait_ge(sems["ld"].h, sems["ld"].n)
                ev_base = sems["ev"].n
                for mi in range(16):
                    ps = ps_mm[mi % 2]
                    if mi >= 2:
                        t.wait_ge(sems["ev"].h, ev_base + mi - 1)
                    ins = None
                    for k in range(NKC):
                        ins = t.matmul(ps[:, :],
                                       w_sb[:, k * D + mi * 128:
                                            k * D + mi * 128 + 128],
                                       rhs[k][:, :],
                                       start=(k == 0), stop=(k == NKC - 1))
                    sems["mm"].inc(ins, 1)

            @blk.vector
            def _(v, tg=tg, sems=sems):
                mm_base = sems["mm"].n - 16
                for mi in range(16):
                    v.wait_ge(sems["mm"].h, mm_base + mi + 1)
                    ins = v.tensor_tensor(sti[mi][:, :], ps_mm[mi % 2][:, :],
                                          xf[mi][:, :], ALU.add)
                    sems["ev"].inc(ins, 1)

            @blk.sync
            def _(sy, tg=tg, sems=sems):
                ev_base = sems["ev"].n - 16
                for mi in range(16):
                    sy.wait_ge(sems["ev"].h, ev_base + mi + 1)
                    sems["st"].inc(sy.dma_start(
                        x2i[mi * 128:mi * 128 + 128, tg * 512:tg * 512 + 512],
                        sti[mi][:, :]), 16)

            st_b = sems["st"].n

            def writer(v, m, tmp, sems=sems, st_b=st_b):
                if m >= 2:
                    v.wait_ge(sems["st"].h, st_b + 16 * (m - 1))
                sems["ev"].inc(v.tensor_copy(outb[m % 2][:, :], tmp), 1)

            emit_ln(p, blk, s_tiles=[sti[k][:, :] for k in range(NKC)],
                    sq_buf=sqb, rows=rows, ones_f32=ones_f32,
                    ones_row=ones_row, eps_tile=eps_t, ps_stat=ps_stat,
                    ps_bc0=ps_bc[0], ps_bc1=ps_bc[1],
                    base_wait=(sems["ev"], sems["ev"].n), writer=writer,
                    gamma_col=lambda m: g2l[:, m:m + 1],
                    beta_col=lambda m: b2l[:, m:m + 1])

            @blk.sync
            def _(sy, tg=tg, sems=sems):
                ev_base = sems["ev"].n - 16
                for m in range(16):
                    sy.wait_ge(sems["ev"].h, ev_base + m + 1)
                    sems["st"].inc(sy.dma_start(
                        xn2[m * 128:m * 128 + 128, tg * 512:tg * 512 + 512],
                        outb[m % 2][:, :]), 16)
                sy.wait_ge(sems["st"].h, sems["st"].n)

        with p.block() as blk:
            semr = {"ld": p.sem("dr_ld"), "mm": p.sem("dr_mm"),
                    "ev": p.sem("dr_ev")}
            @blk.gpsimd
            def _(g, tg=tg, semr=semr):
                for k in range(NKC):
                    semr["ld"].inc(g.dma_start(
                        rhs[k][:, :],
                        xo[k * 128:k * 128 + 128, tg * 512:tg * 512 + 512]), 16)

            @blk.tensor
            def _(t, tg=tg, semr=semr):
                t.wait_ge(semr["ld"].h, semr["ld"].n)
                ins = None
                for k in range(NKC):
                    ins = t.matmul(ps_r[0:3, :], wr_sb[:, k * 3:k * 3 + 3],
                                   xf[k][:, :],
                                   start=(k == 0), stop=(k == NKC - 1))
                semr["mm"].inc(ins, 1)
                for k in range(NKC):
                    ins = t.matmul(ps_mm[0][0:32, :],
                                   wd_sb[:, k * 32:k * 32 + 32], rhs[k][:, :],
                                   start=(k == 0), stop=(k == NKC - 1))
                semr["mm"].inc(ins, 1)

            @blk.vector
            def _(v, tg=tg, semr=semr):
                v.wait_ge(semr["mm"].h, semr["mm"].n)
                v.tensor_copy(lgs[0:3, :], ps_r[0:3, :])
                ins = v.tensor_copy(tts[0:32, :], ps_mm[0][0:32, :])
                semr["ev"].inc(ins, 1)

            @blk.sync
            def _(sy, tg=tg, semr=semr):
                sy.wait_ge(semr["ev"].h, semr["ev"].n)
                semr["ld"].inc(sy.dma_start(
                    lgr_o[0:3, tg * 512:tg * 512 + 512], lgs[0:3, :]), 16)
                semr["ld"].inc(sy.dma_start(
                    tT_o[0:32, tg * 512:tg * 512 + 512], tts[0:32, :]), 16)
                sy.wait_ge(semr["ld"].h, semr["ld"].n)
    return p


def build_d2_launch():
    p = Prog()
    xn2 = p.din("xn2", [D, TOK], BF16)
    ff1T = p.din("ff1T", [D, 8192], F32)
    gt = p.dout("gt", [8192, TOK], BF16)

    xr = p.sb("xr", [128, NKC * TOK], BF16)
    w1b = [[p.sb("w1b", [128, 128], BF16) for _ in range(NKC)]
           for _ in range(2)]
    gs1 = [p.sb("gs1", [128, 512], F32) for _ in range(2)]
    gs2 = [p.sb("gs2", [128, 512], F32) for _ in range(2)]
    gs3 = [p.sb("gs3", [128, 512], F32) for _ in range(2)]
    gout = [p.sb("gout", [128, 512], BF16) for _ in range(2)]
    ps_mm = [p.psum("ps_mm", [128, 512]) for _ in range(2)]

    emit_consts(p, [(xr[:, k * TOK:(k + 1) * TOK],
                     xn2[k * 128:(k + 1) * 128, :], False)
                    for k in range(NKC)])

    wl = p.sem("f1_wl")
    mm = p.sem("f1_mm")
    g1d = p.sem("f1_g1")
    v1d = p.sem("f1_v1")
    t2d = p.sem("f1_t2")
    gv = p.sem("f1_gv")
    st = p.sem("f1_st")

    with p.block() as blk:
        @blk.gpsimd
        def _(g):
            for mf in range(64):
                if mf >= 2:
                    g.wait_ge(mm.h, 4 * (mf - 1))
                for k in range(NKC):
                    wl.inc(g.dma_start(
                        w1b[mf % 2][k][:, :],
                        ff1T[k * 128:k * 128 + 128,
                             mf * 128:mf * 128 + 128]), 16)

        @blk.tensor
        def _(t):
            for mf in range(64):
                t.wait_ge(wl.h, 16 * NKC * (mf + 1))
                for tg in range(4):
                    idx = mf * 4 + tg
                    if idx >= 2:
                        t.wait_ge(gv.h, idx - 1)
                    ins = None
                    for k in range(NKC):
                        ins = t.matmul(
                            ps_mm[idx % 2][:, :], w1b[mf % 2][k][:, :],
                            xr[:, k * TOK + tg * 512:k * TOK + tg * 512 + 512],
                            start=(k == 0), stop=(k == NKC - 1))
                    mm.inc(ins, 1)

        @blk.scalar
        def _(s):
            for idx in range(256):
                s.wait_ge(mm.h, idx + 1)
                if idx >= 2:
                    s.wait_ge(v1d.h, idx - 1)
                g1d.inc(s.square(gs1[idx % 2][:, :], ps_mm[idx % 2][:, :]), 1)
                s.wait_ge(v1d.h, idx + 1)
                t2d.inc(s.activation(gs2[idx % 2][:, :], gs2[idx % 2][:, :],
                                     AF.Tanh, scale=GK), 1)

        @blk.vector
        def _(v):
            for idx in range(256):
                v.wait_ge(g1d.h, idx + 1)
                v.tensor_scalar(gs2[idx % 2][:, :], gs1[idx % 2][:, :],
                                GA, 1.0, ALU.mult, ALU.add)
                v1d.inc(v.tensor_tensor(gs2[idx % 2][:, :], gs2[idx % 2][:, :],
                                        ps_mm[idx % 2][:, :], ALU.mult), 1)
                v.wait_ge(t2d.h, idx + 1)
                if idx >= 2:
                    v.wait_ge(st.h, 16 * (idx - 1))
                v.tensor_scalar(gs3[idx % 2][:, :], gs2[idx % 2][:, :],
                                1.0, 0.5, ALU.add, ALU.mult)
                gv.inc(v.tensor_tensor(gout[idx % 2][:, :], gs3[idx % 2][:, :],
                                       ps_mm[idx % 2][:, :], ALU.mult), 1)

        @blk.sync
        def _(sy):
            for idx in range(256):
                mf, tg = idx // 4, idx % 4
                sy.wait_ge(gv.h, idx + 1)
                st.inc(sy.dma_start(
                    gt[mf * 128:mf * 128 + 128, tg * 512:tg * 512 + 512],
                    gout[idx % 2][:, :]), 16)
            sy.wait_ge(st.h, st.n)
    return p


def build_d3_launch():
    p = Prog()
    NT = 512
    NG = TOK // NT
    gt = p.din("gt", [8192, TOK], BF16)
    x2i = p.din("x2i", [D, TOK], F32)
    xo = p.din("xo", [D, TOK], F32)
    ff2P = p.din("ff2P", [128, 16 * 8192], F32)
    lgr = p.din("lgr", [3, TOK], F32)
    tT_in = p.din("tT", [32, TOK], BF16)
    wu_p = p.din("wu_p", [32, 16 * 128], F32)
    sb_in = p.din("sb_t", [128, 16], F32)
    y = p.dout("y", [D, TOK], F32)

    wu_sb = p.sb("wu_sb", [32, 16 * 128], BF16)
    tts = p.sb("tts", [32, TOK], BF16)
    m0b = p.sb("m0b", [128, TOK], BF16)
    m1b = p.sb("m1b", [128, TOK], BF16)
    m2b = p.sb("m2b", [128, TOK], BF16)
    sbl = p.sb("sbl", [128, 16], F32)
    ones_f32 = p.sb("ones_f32", [128, 1], F32)
    ones_row = p.sb("ones_row", [1, 128], F32)
    eps_t = p.sb("eps_t", [128, 1], F32)
    two_t = p.sb("two_t", [128, 1], F32)
    ps_mm = [p.psum("ps_mm", [128, 512]) for _ in range(2)]
    ps_u = p.psum("ps_u", [128, 512])
    ps_stat = p.psum("ps_stat", [33, 512])
    ps_bc = [p.psum("ps_bc", [128, 512]) for _ in range(2)]

    emit_consts(p, [(wu_sb[:, :], wu_p, True), (tts[:, :], tT_in, False),
                    (sbl[:, :], sb_in, False)],
                [(ones_f32[:, 0:1], 1.0), (ones_row[0:1, :], 1.0),
                 (eps_t[:, 0:1], 1e-5), (two_t[:, 0:1], 2.0)])

    row_stack = ExitStack()
    l0r = p.sb("l0r", [1, TOK], F32, row_stack)
    l1r = p.sb("l1r", [1, TOK], F32, row_stack)
    l2r = p.sb("l2r", [1, TOK], F32, row_stack)
    cr = [p.sb("cr", [1, TOK], F32, row_stack) for _ in range(2)]
    m0r = p.sb("m0r", [1, TOK], F32, row_stack)
    m1r = p.sb("m1r", [1, TOK], F32, row_stack)
    m2r = p.sb("m2r", [1, TOK], F32, row_stack)
    emit_consts(p, [(l0r[0:1, :], lgr[0:1, :], False),
                    (l1r[0:1, :], lgr[1:2, :], False),
                    (l2r[0:1, :], lgr[2:3, :], False)])
    cm = p.sem("d3_cm")
    bcs = p.sem("d3_bc")
    cps = p.sem("d3_cp")
    with p.block() as blk:
        @blk.vector
        def _(v):
            v.tensor_tensor(cr[0][0:1, :], l0r[0:1, :], l1r[0:1, :], ALU.is_ge)
            v.tensor_tensor(cr[1][0:1, :], l0r[0:1, :], l2r[0:1, :], ALU.is_ge)
            v.tensor_tensor(m0r[0:1, :], cr[0][0:1, :], cr[1][0:1, :], ALU.mult)
            v.tensor_tensor(cr[0][0:1, :], l1r[0:1, :], l0r[0:1, :], ALU.is_gt)
            v.tensor_tensor(cr[1][0:1, :], l1r[0:1, :], l2r[0:1, :], ALU.is_ge)
            v.tensor_tensor(m1r[0:1, :], cr[0][0:1, :], cr[1][0:1, :], ALU.mult)
            v.tensor_tensor(cr[0][0:1, :], l2r[0:1, :], l0r[0:1, :], ALU.is_gt)
            v.tensor_tensor(cr[1][0:1, :], l2r[0:1, :], l1r[0:1, :], ALU.is_gt)
            ins = v.tensor_tensor(m2r[0:1, :], cr[0][0:1, :], cr[1][0:1, :],
                                  ALU.mult)
            cm.inc(ins, 1)

        @blk.tensor
        def _(t):
            t.wait_ge(cm.h, cm.n)
            cp_base = cps.n
            for tg in range(4):
                if tg >= 1:
                    t.wait_ge(cps.h, cp_base + tg)
                sl = slice(tg * 512, tg * 512 + 512)
                t.matmul(ps_mm[0][:, :], ones_row[0:1, :], m0r[0:1, sl],
                         start=True, stop=True)
                t.matmul(ps_mm[1][:, :], ones_row[0:1, :], m1r[0:1, sl],
                         start=True, stop=True)
                ins = t.matmul(ps_bc[0][:, :], ones_row[0:1, :], m2r[0:1, sl],
                               start=True, stop=True)
                bcs.inc(ins, 1)

        @blk.vector
        def _(v):
            bc_base = bcs.n - 4
            for tg in range(4):
                v.wait_ge(bcs.h, bc_base + tg + 1)
                sl = slice(tg * 512, tg * 512 + 512)
                v.tensor_copy(m0b[:, sl], ps_mm[0][:, :])
                v.tensor_copy(m1b[:, sl], ps_mm[1][:, :])
                ins = v.tensor_copy(m2b[:, sl], ps_bc[0][:, :])
                cps.inc(ins, 1)

    row_stack.close()

    gtb = [p.sb("gtb", [128, NT], BF16) for _ in range(64)]
    w2b = [p.sb("w2b", [128, 64 * 128], BF16) for _ in range(2)]
    hf = [p.sb("hf", [128, NT], F32) for _ in range(2)]
    x2f = [p.sb("x2f", [128, NT], F32) for _ in range(2)]
    sti = [p.sb("sti", [128, NT], F32) for _ in range(NKC)]
    bl1 = p.sb("bl1", [128, NT], F32)
    bl2 = p.sb("bl2", [128, NT], F32)
    sqb = [p.sb("ln_sq", [128, NT], F32) for _ in range(2)]
    rows = {k: p.sb("ln_rows", [33, NT], F32)
            for k in ("mrow", "vrow", "m2", "rr")}
    outb = [p.sb("outb", [128, NT], F32) for _ in range(2)]

    for g_i in range(NG):
        sems = {"ld": p.sem("d3_ld"), "wl": p.sem("d3_wl"),
                "mm": p.sem("d3_mm"), "ev": p.sem("d3_ev"),
                "st": p.sem("d3_st"), "xl": p.sem("d3_xl")}
        c0 = g_i * NT
        with p.block() as blk:
            @blk.sync
            def _(sy, c0=c0, sems=sems):
                for k in range(64):
                    sems["ld"].inc(sy.dma_start(
                        gtb[k][:, :], gt[k * 128:k * 128 + 128, c0:c0 + NT]),
                        16)
                xl_base = sems["xl"].n
                ev_base = sems["ev"].n
                for mi in range(16):
                    if mi >= 2:
                        sy.wait_ge(sems["ev"].h, ev_base + mi - 1)
                    sems["xl"].inc(sy.dma_start(
                        hf[mi % 2][:, :],
                        xo[mi * 128:mi * 128 + 128, c0:c0 + NT]), 16)
                    sems["xl"].inc(sy.dma_start(
                        x2f[mi % 2][:, :],
                        x2i[mi * 128:mi * 128 + 128, c0:c0 + NT]), 16)

            @blk.gpsimd
            def _(g, c0=c0, sems=sems):
                mm_base = sems["mm"].n
                for mi in range(16):
                    if mi >= 2:
                        g.wait_ge(sems["mm"].h, mm_base + mi - 1)
                    sems["wl"].inc(g.dma_start(
                        w2b[mi % 2][:, :],
                        ff2P[:, mi * 8192:(mi + 1) * 8192]), 16)

            @blk.tensor
            def _(t, c0=c0, sems=sems):
                t.wait_ge(sems["ld"].h, sems["ld"].n)
                ev_base = sems["ev"].n
                wl_base = sems["wl"].n - 16 * 16
                for mi in range(16):
                    t.wait_ge(sems["wl"].h, wl_base + 16 * (mi + 1))
                    ps = ps_mm[mi % 2]
                    if mi >= 1:
                        t.wait_ge(sems["ev"].h, ev_base + mi)
                    for k in range(64):
                        t.matmul(ps[:, 0:NT],
                                 w2b[mi % 2][:, k * 128:k * 128 + 128],
                                 gtb[k][:, :],
                                 start=(k == 0), stop=(k == 63))
                    ins = t.matmul(ps_u[:, 0:NT],
                                   wu_sb[:, mi * 128:mi * 128 + 128],
                                   tts[0:32, c0:c0 + NT],
                                   start=True, stop=True)
                    sems["mm"].inc(ins, 1)

            @blk.vector
            def _(v, c0=c0, sems=sems):
                mm_base = sems["mm"].n - 16
                xl_base = sems["xl"].n - 32 * 16
                for mi in range(16):
                    v.wait_ge(sems["mm"].h, mm_base + mi + 1)
                    v.wait_ge(sems["xl"].h, xl_base + 32 * (mi + 1))
                    ps = ps_mm[mi % 2]
                    sl = slice(c0, c0 + NT)
                    v.tensor_tensor(bl1[:, :], ps[:, 0:NT], x2f[mi % 2][:, :],
                                    ALU.add)
                    v.tensor_tensor(bl1[:, :], bl1[:, :], hf[mi % 2][:, :],
                                    ALU.add)
                    v.tensor_tensor(sti[mi][:, :], bl1[:, :], m2b[:, sl],
                                    ALU.mult)
                    v.scalar_tensor_tensor(bl2[:, :], hf[mi % 2][:, :],
                                           two_t[:, 0:1], ps_u[:, 0:NT],
                                           ALU.mult, ALU.add)
                    v.tensor_tensor(bl2[:, :], bl2[:, :], m1b[:, sl], ALU.mult)
                    v.tensor_tensor(sti[mi][:, :], sti[mi][:, :], bl2[:, :],
                                    ALU.add)
                    v.tensor_scalar(bl2[:, :], hf[mi % 2][:, :], 2.0,
                                    sbl[:, mi:mi + 1], ALU.mult, ALU.add)
                    v.tensor_tensor(bl2[:, :], bl2[:, :], m0b[:, sl], ALU.mult)
                    ins = v.tensor_tensor(sti[mi][:, :], sti[mi][:, :],
                                          bl2[:, :], ALU.add)
                    sems["ev"].inc(ins, 1)

            st_b = sems["st"].n

            def writer(v, m, tmp, sems=sems, st_b=st_b):
                if m >= 2:
                    v.wait_ge(sems["st"].h, st_b + 16 * (m - 1))
                sems["ev"].inc(v.tensor_copy(outb[m % 2][:, :], tmp), 1)

            emit_ln(p, blk, s_tiles=[sti[k][:, :] for k in range(NKC)],
                    sq_buf=sqb, rows=rows, ones_f32=ones_f32,
                    ones_row=ones_row, eps_tile=eps_t, ps_stat=ps_stat,
                    ps_bc0=ps_bc[0], ps_bc1=ps_bc[1],
                    base_wait=(sems["ev"], sems["ev"].n), writer=writer,
                    ntok=NT)

            @blk.sync
            def _(sy, c0=c0, sems=sems):
                ev_base = sems["ev"].n - 16
                for m in range(16):
                    sy.wait_ge(sems["ev"].h, ev_base + m + 1)
                    sems["st"].inc(sy.dma_start(
                        y[m * 128:m * 128 + 128, c0:c0 + NT],
                        outb[m % 2][:, :]), 16)
                sy.wait_ge(sems["st"].h, sems["st"].n)
    return p



_PROG_CACHE = {}


def _get_prog(key, builder):
    if key not in _PROG_CACHE:
        _PROG_CACHE[key] = builder()
    return _PROG_CACHE[key]


def _run(p, in_maps):
    res = run_bass_kernel_spmd(p.nc, in_maps, core_ids=list(range(8)))
    return res.results


def kernel(**inp):
    f32 = np.float32
    xT = [np.ascontiguousarray(np.asarray(inp["L0_attn_x"][b], f32).T)
          for b in range(B)]

    pA = _get_prog("A", lambda: build_attn_launch("l0a"))
    wq = np.asarray(inp["L0_attn_wq"], f32)
    wk = np.asarray(inp["L0_attn_wk"], f32)
    wv = np.asarray(inp["L0_attn_wv"], f32)
    bq = np.asarray(inp["L0_attn_bq"], f32)
    bk = np.asarray(inp["L0_attn_bk"], f32)
    bv = np.asarray(inp["L0_attn_bv"], f32)
    in_maps = []
    for c in range(8):
        b, lane = c // 4, c % 4
        hs = slice(lane * 512, lane * 512 + 512)
        ks = slice(lane * 128, lane * 128 + 128)
        qcos, qsin = _rope_cs([lane * 4 + j for j in range(4)])
        kcos, ksin = _rope_cs([lane])
        in_maps.append({
            "xt": xT[b],
            "wq_p": _pack_lhst(wq.T[:, hs]),
            "wk_p": _pack_lhst(wk.T[:, ks]),
            "wv_p": _pack_lhst(wv.T[:, ks]),
            "bq_t": _col_table(bq[hs], 4),
            "bk_t": _col_table(bk[ks], 1),
            "bvb": np.ascontiguousarray(
                np.broadcast_to(bv[ks][None, :], (128, 128)).astype(f32)),
            "rq_cos": qcos, "rq_sin": qsin,
            "rk_cos": kcos, "rk_sin": ksin,
            "eye": _EYE, "maske": _MASK_E, "mask01": _MASK01,
        })
    resA = _run(pA, in_maps)
    ctxa = [np.concatenate([resA[b * 4 + l]["ctxT"] for l in range(4)], 0)
            for b in range(B)]

    pB = _get_prog("PJ", lambda: build_projln_launch("proj"))
    wo_p = _pack_lhst(np.asarray(inp["L0_attn_wo"], f32).T)
    bo_t = _col_table(np.asarray(inp["L0_attn_bo"], f32), 16)
    in_maps = []
    for c in range(8):
        b, lane = c // 4, c % 4
        own = slice(lane * TOK, lane * TOK + TOK)
        in_maps.append({
            "xo": np.ascontiguousarray(xT[b][:, own]),
            "cin": np.ascontiguousarray(ctxa[b][:, own]),
            "w_p": wo_p, "b_t": bo_t,
        })
    resB = _run(pB, in_maps)
    x1T = [np.concatenate([resB[b * 4 + l]["y"] for l in range(4)], 1)
           for b in range(B)]

    pC = _get_prog("C", lambda: build_attn_launch("inner"))
    wqi = np.asarray(inp["L0_ffn_wq"], f32)
    wki = np.asarray(inp["L0_ffn_wk"], f32)
    wvi = np.asarray(inp["L0_ffn_wv"], f32)
    g1_t = _col_table(np.asarray(inp["L0_ffn_ln1_scale"], f32), 16)
    b1_t = _col_table(np.asarray(inp["L0_ffn_ln1_bias"], f32), 16)
    in_maps = []
    for c in range(8):
        b, lane = c // 4, c % 4
        hs = slice(lane * 512, lane * 512 + 512)
        in_maps.append({
            "xt": x1T[b],
            "wq_p": _pack_lhst(wqi.T[:, hs]),
            "wk_p": _pack_lhst(wki.T[:, hs]),
            "wv_p": _pack_lhst(wvi.T[:, hs]),
            "g1_t": g1_t, "b1_t": b1_t,
            "eye": _EYE, "maske": _MASK_E, "mask01": _MASK01,
        })
    resC = _run(pC, in_maps)
    ctxi = [np.concatenate([resC[b * 4 + l]["ctxT"] for l in range(4)], 0)
            for b in range(B)]

    pD1 = _get_prog("D1", lambda: build_d1_launch())
    woi_p = _pack_lhst(np.asarray(inp["L0_ffn_wo"], f32).T)
    g2_t = _col_table(np.asarray(inp["L0_ffn_ln2_scale"], f32), 16)
    b2_t = _col_table(np.asarray(inp["L0_ffn_ln2_bias"], f32), 16)
    wr_p = _pack_lhst(np.asarray(inp["L0_ffn_w_router"], f32).T)
    wd_p = _pack_lhst(np.asarray(inp["L0_ffn_w_down"], f32).T)
    in_maps = []
    for c in range(8):
        b, lane = c // 4, c % 4
        own = slice(lane * TOK, lane * TOK + TOK)
        in_maps.append({
            "xo": np.ascontiguousarray(x1T[b][:, own]),
            "cin": np.ascontiguousarray(ctxi[b][:, own]),
            "wo_p": woi_p, "g2_t": g2_t, "b2_t": b2_t,
            "wr_p": wr_p, "wd_p": wd_p,
        })
    resD1 = _run(pD1, in_maps)

    pD2 = _get_prog("D2", lambda: build_d2_launch())
    ff1T_h = np.ascontiguousarray(np.asarray(inp["L0_ffn_w_ff1"], f32).T)
    in_maps = [{"xn2": resD1[c]["xn2"], "ff1T": ff1T_h} for c in range(8)]
    resD2 = _run(pD2, in_maps)

    pD3 = _get_prog("D3", lambda: build_d3_launch())
    ff2P_h = np.ascontiguousarray(
        np.asarray(inp["L0_ffn_w_ff2"], f32).T.reshape(64, 128, 16, 128)
        .transpose(1, 2, 0, 3).reshape(128, 16 * 8192))
    wu_p = np.ascontiguousarray(
        np.asarray(inp["L0_ffn_w_up"], f32).T.reshape(32, 16 * 128))
    sb_t = _col_table(np.asarray(inp["L0_ffn_simple_bias"], f32), 16)
    in_maps = []
    for c in range(8):
        b, lane = c // 4, c % 4
        own = slice(lane * TOK, lane * TOK + TOK)
        in_maps.append({
            "gt": resD2[c]["gt"], "x2i": resD1[c]["x2i"],
            "xo": np.ascontiguousarray(x1T[b][:, own]),
            "ff2P": ff2P_h, "lgr": resD1[c]["lgr"], "tT": resD1[c]["tT"],
            "wu_p": wu_p, "sb_t": sb_t,
        })
    resD3 = _run(pD3, in_maps)
    x2T = [np.concatenate([resD3[b * 4 + l]["y"] for l in range(4)], 1)
           for b in range(B)]

    pE = _get_prog("E", lambda: build_attn_launch("l1"))
    Wq = np.asarray(inp["L1_attn_Wq"], f32)
    Wk = np.asarray(inp["L1_attn_Wk"], f32)
    Wv = np.asarray(inp["L1_attn_Wv"], f32)
    bq1 = np.asarray(inp["L1_attn_bq"], f32)
    bk1 = np.asarray(inp["L1_attn_bk"], f32)
    bv1 = np.asarray(inp["L1_attn_bv"], f32)
    rp_cos, rp_sin = _rope_cs_swapped(np.arange(S))
    in_maps = []
    for c in range(8):
        b, lane = c // 4, c % 4
        hs = slice(lane * 512, lane * 512 + 512)
        in_maps.append({
            "xt": x2T[b],
            "wq_p": _pack_lhst(Wq.T[:, hs]),
            "wk_p": _pack_lhst(Wk.T[:, hs]),
            "wv_p": _pack_lhst(Wv.T[:, hs]),
            "bq_t": _col_table(bq1[hs], 4),
            "bk_t": _col_table(bk1[hs], 4),
            "bvb": np.ascontiguousarray(
                np.broadcast_to(bv1[hs][None, :], (128, 512)).astype(f32)),
            "rp_cos": rp_cos, "rp_sin": rp_sin,
            "eye": _EYE, "maske": _MASK_E, "mask01": _MASK01,
        })
    resE = _run(pE, in_maps)
    ctxl = [np.concatenate([resE[b * 4 + l]["ctxT"] for l in range(4)], 0)
            for b in range(B)]

    wo1_p = _pack_lhst(np.asarray(inp["L1_attn_Wo"], f32).T)
    bo1_t = _col_table(np.asarray(inp["L1_attn_bo"], f32), 16)
    in_maps = []
    for c in range(8):
        b, lane = c // 4, c % 4
        own = slice(lane * TOK, lane * TOK + TOK)
        in_maps.append({
            "xo": np.ascontiguousarray(x2T[b][:, own]),
            "cin": np.ascontiguousarray(ctxl[b][:, own]),
            "w_p": wo1_p, "b_t": bo1_t,
        })
    resF = _run(pB, in_maps)
    x3T = [np.concatenate([resF[b * 4 + l]["y"] for l in range(4)], 1)
           for b in range(B)]

    pG = _get_prog("G", lambda: build_projln_launch("ffn2"))
    wt_p = _pack_lhst(np.asarray(inp["L1_ffn_weight"], f32).T)
    s_t = _col_table(np.asarray(inp["L1_ffn_scale_weight"], f32), 16)
    bL_t = _col_table(np.asarray(inp["L1_ffn_bias"], f32), 16)
    in_maps = []
    for c in range(8):
        b, lane = c // 4, c % 4
        own = slice(lane * TOK, lane * TOK + TOK)
        in_maps.append({
            "xo": np.ascontiguousarray(x3T[b][:, own]),
            "w_p": wt_p, "s_t": s_t, "b_t": bL_t,
        })
    resG = _run(pG, in_maps)

    out = np.empty((B, S, D), np.float32)
    for c in range(8):
        b, lane = c // 4, c % 4
        out[b, lane * TOK:(lane + 1) * TOK, :] = resG[c]["y"].T
    return out
